# version 15
# baseline (speedup 1.0000x reference)
import numpy as np
import ml_dtypes
from contextlib import ExitStack

import concourse.bass as bass
import concourse.mybir as mybir
from concourse.bass_utils import run_bass_kernel_spmd

F32 = mybir.dt.float32
BF16 = mybir.dt.bfloat16
I32 = mybir.dt.int32
AF = mybir.ActivationFunctionType
ALU = mybir.AluOpType
AX = mybir.AxisListType

P = 128
N = 4096
T = 2048
D = 1024
H = 3072
KC = D // P
HC = H // P
NT_ALL = N // P
NT_OWN = T // P
EPS = 1e-5
INV32 = 1.0 / 32.0
INV127 = 1.0 / 127.0


class Sem:
    def __init__(self, ctx, nc, name):
        self.h = ctx.enter_context(nc.semaphore(name))
        self.n = 0


def build_program():
    nc = bass.Bass("TRN2", target_bir_lowering=False)

    x_d = nc.dram_tensor("x", [N, D], F32, kind="ExternalInput").ap()
    wspT_d = nc.dram_tensor("wspT", [D, D], F32, kind="ExternalInput").ap()
    wjT_d = nc.dram_tensor("wjT", [D, D], F32, kind="ExternalInput").ap()
    wglT_d = nc.dram_tensor("wglT", [D, D], F32, kind="ExternalInput").ap()
    bbc_d = nc.dram_tensor("b_bc", [P, D], F32, kind="ExternalInput").ap()
    beta_d = nc.dram_tensor("beta_col", [P, 1], F32, kind="ExternalInput").ap()
    ident_d = nc.dram_tensor("ident", [P, P], BF16, kind="ExternalInput").ap()
    wgT_d = nc.dram_tensor("wgT", [D, H], I32, kind="ExternalInput").ap()
    wuT_d = nc.dram_tensor("wuT", [D, H], I32, kind="ExternalInput").ap()
    wdT_d = nc.dram_tensor("wdT", [H, D], I32, kind="ExternalInput").ap()
    out_d = nc.dram_tensor("out", [T, D], F32, kind="ExternalOutput").ap()
    attn_d = nc.dram_tensor("attn_sc", [T, D], BF16, kind="ExternalOutput").ap()
    x1_d = nc.dram_tensor("x1_sc", [T, D], F32, kind="ExternalOutput").ap()
    e_d = nc.dram_tensor("e_sc", [NT_OWN, P, KC, P], BF16,
                         kind="ExternalOutput").ap()

    ctx = ExitStack()
    with ctx:
        e = ctx.enter_context

        da = [Sem(ctx, nc, "da0"), Sem(ctx, nc, "da1")]
        db = Sem(ctx, nc, "db")
        do_ = [Sem(ctx, nc, "do0"), Sem(ctx, nc, "do1")]
        dsm = [Sem(ctx, nc, "ds0"), Sem(ctx, nc, "ds1")]
        dtm = [Sem(ctx, nc, "dt0"), Sem(ctx, nc, "dt1")]
        dwA = Sem(ctx, nc, "dwA")
        dwB = Sem(ctx, nc, "dwB")
        dwC = Sem(ctx, nc, "dwC")
        dwP = [Sem(ctx, nc, "dwD"), Sem(ctx, nc, "dwE")]
        dwF = Sem(ctx, nc, "dwF")
        dtp = [Sem(ctx, nc, "dtp0"), Sem(ctx, nc, "dtp1")]
        dts = [Sem(ctx, nc, "dts0"), Sem(ctx, nc, "dts1")]
        deG = [Sem(ctx, nc, "deG0"), Sem(ctx, nc, "deG1")]
        deL = [Sem(ctx, nc, "deL0"), Sem(ctx, nc, "deL1")]
        sv = Sem(ctx, nc, "sv")
        ss = Sem(ctx, nc, "ss")
        st = Sem(ctx, nc, "st")

        def vinc(sem, instr, k=1):
            instr.then_inc(sem.h, k)
            sem.n += k
            return sem.n

        ident = e(nc.sbuf_tensor("ident_sb", [P, P], BF16))
        beta_col = e(nc.sbuf_tensor("beta_sb", [P, 1], F32))
        ones_col = e(nc.sbuf_tensor("ones_col", [P, 1], BF16))
        ones_row = e(nc.sbuf_tensor("ones_row", [1, P], BF16))
        g_bc = e(nc.sbuf_tensor("g_bc", [P, D], F32))
        eps_col = e(nc.sbuf_tensor("eps_col", [P, 1], F32))
        c_inv32 = e(nc.sbuf_tensor("c_inv32", [P, 1], F32))
        c_sc127 = e(nc.sbuf_tensor("c_sc127", [P, 1], F32))
        c_b127 = e(nc.sbuf_tensor("c_b127", [P, 1], F32))
        mv2_all = e(nc.sbuf_tensor("mv2_all", [P, NT_OWN, 2], F32))

        block = e(nc.Block())
        rec = {}
        sdefer = []

        def s_add(fn):
            sdefer.append(fn)

        def s_flush():
            fns = list(sdefer)
            del sdefer[:]
            if fns:
                def run(s, fns=fns):
                    for f in fns:
                        f(s)
                block.scalar(run)

        def boot_vec(vector):
            vector.memset(ones_col[:], 1.0)
            vector.memset(ones_row[:], 1.0)
            vector.memset(eps_col[:], EPS)
            vector.memset(c_inv32[:], INV32)
            vector.memset(c_sc127[:], 127.0 * 127.0)
            m = vector.memset(c_b127[:], 127.0 * 127.0 * EPS)
            vinc(sv, m)

        ctxA = ExitStack()
        eA = ctxA.enter_context
        s_tok = eA(nc.sbuf_tensor("s_tok", [P, NT_ALL, D], BF16))
        s_FM = eA(nc.sbuf_tensor("s_FM", [P, KC, N], BF16))

        ctxA1 = ExitStack()
        e1 = ctxA1.enter_context
        wspT_sb = e1(nc.sbuf_tensor("wspT_sb", [P, KC, D], BF16))
        wjT_sb = e1(nc.sbuf_tensor("wjT_sb", [P, KC, D], BF16))
        e_stage = e1(nc.sbuf_tensor("e_stage", [P, 2, 512], BF16))
        b_row = e1(nc.sbuf_tensor("b_row", [1, D], BF16))
        xs = e1(nc.sbuf_tensor("xs", [P, 2, 2, D], F32))
        xn_bf = e1(nc.sbuf_tensor("xn_bf", [P, 2, D], BF16))
        xnT_sb = e1(nc.sbuf_tensor("xnT_sb", [P, 2, D], BF16))
        st6 = e1(nc.sbuf_tensor("st6", [P, 2, 12], F32))
        mvb = e1(nc.sbuf_tensor("mvb", [P, 2, 2], F32))
        sqv = e1(nc.sbuf_tensor("sqv", [P, 2, 1], F32))
        rsig = e1(nc.sbuf_tensor("rsig", [P, 2, 1], F32))
        ps_s = e1(nc.psum_tensor("ps_s", [P, 2, D], F32))
        ps_e = e1(nc.psum_tensor("ps_e", [P, 2, 512], F32))

        def boot_sync(sync):
            sync.dma_start(out=ident[:], in_=ident_d).then_inc(db.h, 16)
            sync.dma_start(out=beta_col[:], in_=beta_d).then_inc(db.h, 16)
        block.sync(boot_sync)
        db.n += 32
        block.vector(boot_vec)
        sv_ones = sv.n

        def boot_gp(gp):
            gp.dma_start(out=wspT_sb[:],
                         in_=wspT_d.rearrange("(j p) d -> p j d", p=P)
                         ).then_inc(dwA.h, 16)
            gp.dma_start(out=b_row[:], in_=bbc_d[0:1, :]).then_inc(dwA.h, 16)
            gp.dma_start(out=wjT_sb[:],
                         in_=wjT_d.rearrange("(j p) d -> p j d", p=P)
                         ).then_inc(dwB.h, 16)
        block.gpsimd(boot_gp)
        dwA.n += 32
        dwB.n += 16

        x_r = x_d.rearrange("(c t p) d -> c p t d", t=2, p=P)

        def emit_xchunk(c):
            sem = da[c % 2]
            w_ss = rec.get(("apply", 2 * c - 3), 0)

            def f(sync, c=c, sem=sem, w_ss=w_ss):
                if w_ss:
                    sync.wait_ge(sv.h, w_ss)
                sync.dma_start(out=xs[:, c % 2], in_=x_r[c]).then_inc(sem.h, 16)
            block.sync(f)
            sem.n += 16
            rec[("xchunk", c)] = (sem, sem.n)

        emit_xchunk(0)
        emit_xchunk(1)

        def pa_stats(k):
            c, par = k // 2, k % 2
            xsl = xs[:, c % 2, par, :]
            w_sq = rec.get(("sqrt", k - 2), 0)

            def fvA(v, k=k, c=c, par=par, xsl=xsl, w_sq=w_sq):
                w = rec[("xchunk", c)]
                v.wait_ge(w[0].h, w[1])
                if w_sq:
                    v.wait_ge(ss.h, w_sq)
                v.bn_stats(out=st6[:, par, 0:6], in_=xsl[:, 0:512])
                i = v.bn_stats(out=st6[:, par, 6:12], in_=xsl[:, 512:1024])
                vinc(sv, i)
                v.wait_ge(sv.h, sv.n)
                i = v.bn_aggr(out=mvb[:, par, :], in_=st6[:, par, :])
                vinc(sv, i)
            block.vector(fvA)
            rec[("statsA", k)] = sv.n

            def fsq(s, k=k, par=par, wA=sv.n):
                s.wait_ge(sv.h, wA)
                s.activation(out=sqv[:, par, :], in_=mvb[:, par, 1:2],
                             func=AF.Sqrt, bias=eps_col[:]).then_inc(ss.h, 1)
            s_add(fsq)
            ss.n += 1
            rec[("sqrt", k)] = ss.n

        def pa_apply(k):
            c, par = k // 2, k % 2
            xsl = xs[:, c % 2, par, :]
            w_tr = rec.get(("xntr", k - 2), 0)

            def fvB(v, k=k, par=par, xsl=xsl, w_tr=w_tr):
                v.wait_ge(ss.h, rec[("sqrt", k)])
                i = v.reciprocal(out=rsig[:, par, :], in_=sqv[:, par, :])
                vinc(sv, i)
                v.wait_ge(sv.h, sv.n)
                if w_tr:
                    v.wait_ge(w_tr[0].h, w_tr[1])
                i = v.tensor_scalar(out=xn_bf[:, par, :], in0=xsl,
                                    scalar1=mvb[:, par, 0:1],
                                    scalar2=rsig[:, par, :],
                                    op0=ALU.subtract, op1=ALU.mult)
                vinc(sv, i)
            block.vector(fvB)
            rec[("apply", k)] = sv.n

        def pa_xntr(k):
            par = k % 2
            sem = dtp[par]
            w_mm = rec.get(("mm", k - 2), 0)

            def fdt(s, k=k, par=par, sem=sem, wv=rec[("apply", k)], w_mm=w_mm):
                s.wait_ge(sv.h, wv)
                if w_mm:
                    s.wait_ge(st.h, w_mm)
                s.dma_start_transpose(
                    out=xnT_sb[:, par, :].rearrange("p (j q) -> p j q", j=KC),
                    in_=xn_bf[:, par, :]).then_inc(sem.h, 16)
            s_add(fdt)
            sem.n += 16
            rec[("xntr", k)] = (sem, sem.n)

        def pa_mm(k):
            par = k % 2
            w_bias = rec.get(("biasS", k - 2), 0)

            def ft2(t, k=k, par=par, w_bias=w_bias):
                w = rec[("xntr", k)]
                t.wait_ge(w[0].h, w[1])
                if k == 0:
                    t.wait_ge(dwA.h, 32)
                    t.wait_ge(sv.h, sv_ones)
                if w_bias:
                    t.wait_ge(ss.h, w_bias)
                for nb in range(2):
                    t.matmul(ps_s[:, par, nb * 512:(nb + 1) * 512],
                             ones_row[:], b_row[0:1, nb * 512:(nb + 1) * 512],
                             start=True, stop=False)
                    for j in range(KC):
                        i = t.matmul(ps_s[:, par, nb * 512:(nb + 1) * 512],
                                     xnT_sb[:, par, j * P:(j + 1) * P],
                                     wspT_sb[:, j, nb * 512:(nb + 1) * 512],
                                     start=False, stop=(j == KC - 1))
                vinc(st, i)
            block.tensor(ft2)
            rec[("mm", k)] = st.n

        def pa_bias(k):
            def fsb(s, k=k, par=k % 2, wt=rec[("mm", k)]):
                s.wait_ge(st.h, wt)
                s.copy(out=s_tok[:, k, :], in_=ps_s[:, par, :]).then_inc(ss.h, 1)
            s_add(fsb)
            ss.n += 1
            rec[("biasS", k)] = ss.n

        def pa_stra(k):
            sem = dts[k % 2]

            def fdt(s, k=k, sem=sem, w_self=sem.n):
                s.wait_ge(ss.h, rec[("biasS", k)])
                s.wait_ge(sem.h, w_self)
                s.dma_start_transpose(
                    out=s_FM[:, :, k * P:(k + 1) * P],
                    in_=s_tok[:, k, :]).then_inc(sem.h, 16)
            s_add(fdt)
            sem.n += 16
            rec[("stra", k)] = (sem, sem.n)
            if k % 4 == 3:
                rec[("stra4", k // 4)] = (dts[0].n, dts[1].n)

        e_sc_w = e_d.rearrange("n p j t -> p n j t")
        eq_groups = [(m, tb) for tb in range(4) for m in range(KC)]
        eg_state = {"i": 0}
        pending_evac = []

        def emit_energy_mm(gi):
            m, tb = eq_groups[gi]
            slot = gi % 2

            def fte(t, m=m, tb=tb, slot=slot, gi=gi):
                if gi == 0:
                    t.wait_ge(dwB.h, 16)
                w4 = rec[("stra4", tb)]
                t.wait_ge(dts[0].h, w4[0])
                t.wait_ge(dts[1].h, w4[1])
                if gi >= 2:
                    t.wait_ge(sv.h, rec[("e_evac", gi - 2)])
                for j in range(KC):
                    i = t.matmul(ps_e[:, slot, :],
                                 wjT_sb[:, j, m * P:(m + 1) * P],
                                 s_FM[:, j, tb * 512:(tb + 1) * 512],
                                 start=(j == 0), stop=(j == KC - 1))
                vinc(st, i)
            block.tensor(fte)
            rec[("e_mm", gi)] = st.n

        def emit_energy_evac(gi):
            m, tb = eq_groups[gi]
            slot = gi % 2
            w_wr = rec.get(("e_wr", gi - 2), 0)

            def fve(v, m=m, tb=tb, slot=slot, gi=gi, w_wr=w_wr):
                v.wait_ge(st.h, rec[("e_mm", gi)])
                if w_wr:
                    v.wait_ge(w_wr[0].h, w_wr[1])
                i = v.tensor_copy(out=e_stage[:, slot, :], in_=ps_e[:, slot, :])
                vinc(sv, i)
            block.vector(fve)
            rec[("e_evac", gi)] = sv.n

            sem = deG[gi % 2]

            def fgw(gp, m=m, tb=tb, slot=slot, gi=gi, sem=sem):
                gp.wait_ge(sv.h, rec[("e_evac", gi)])
                gp.dma_start(
                    out=e_sc_w[:, 4 * tb:4 * tb + 4, m, :],
                    in_=e_stage[:, slot, :].rearrange("p (a t) -> p a t", a=4)
                    ).then_inc(sem.h, 16)
            block.gpsimd(fgw)
            sem.n += 16
            rec[("e_wr", gi)] = (sem, sem.n)
            if m == KC - 1:
                rec[("e_tb", tb)] = (deG[0].n, deG[1].n)

        for k in range(NT_ALL + 2):
            if k < NT_ALL:
                pa_stats(k)
            if 1 <= k < NT_ALL + 1:
                pa_apply(k - 1)
                pa_xntr(k - 1)
            if 2 <= k:
                pa_bias(k - 2)
                pa_stra(k - 2)
            for gi in pending_evac:
                emit_energy_evac(gi)
            del pending_evac[:]
            if 1 <= k < NT_ALL + 1:
                pa_mm(k - 1)
                if (k - 1) % 2 == 1:
                    c2 = (k - 1) // 2 + 2
                    if c2 < NT_ALL // 2:
                        emit_xchunk(c2)
            while (eg_state["i"] < len(eq_groups)
                   and ("stra4", eq_groups[eg_state["i"]][1]) in rec
                   and len(pending_evac) < 2
                   and eg_state["i"] < max(0, (k - 4)) * 2):
                emit_energy_mm(eg_state["i"])
                pending_evac.append(eg_state["i"])
                eg_state["i"] += 1
        for gi in pending_evac:
            emit_energy_evac(gi)
        del pending_evac[:]
        while eg_state["i"] < len(eq_groups):
            emit_energy_mm(eg_state["i"])
            emit_energy_evac(eg_state["i"])
            eg_state["i"] += 1
        s_flush()
        ctxA1.close()
        sv_pa_done = rec[("apply", NT_ALL - 1)]
        ss_pa_done = rec[("biasS", NT_ALL - 1)]
        pa_tr_done = [(s_, s_.n) for s_ in (dtp[0], dtp[1], dts[0], dts[1])]
        pa_done_st = rec[("mm", NT_ALL - 1)]

        sv_e_done = rec[("e_evac", KC * 4 - 1)]

        ctxC = ExitStack()
        eC = ctxC.enter_context
        P_sb = eC(nc.sbuf_tensor("P_sb", [P, N], BF16))
        PT_sb = eC(nc.sbuf_tensor("PT_sb", [P, 2, N], BF16))
        qm4 = eC(nc.sbuf_tensor("qm4", [P, 2, 4], F32))
        m_col = eC(nc.sbuf_tensor("m_col", [P, 2, 1], F32))
        biasq = eC(nc.sbuf_tensor("biasq", [P, 2, 4], F32))
        dcol = eC(nc.sbuf_tensor("dcol", [P, 2, 4], F32))
        cfac = eC(nc.sbuf_tensor("cfac", [P, 2, 4], F32))
        rsq = eC(nc.sbuf_tensor("rsq", [P, 2, 4], F32))
        rstot = eC(nc.sbuf_tensor("rstot", [P, 2, 1], F32))
        rinv = eC(nc.sbuf_tensor("rinv", [P, NT_OWN], F32))
        tmp1 = eC(nc.sbuf_tensor("tmp1", [P, D], F32))
        tmp2 = eC(nc.sbuf_tensor("tmp2", [P, D], F32))
        attn_wb = eC(nc.sbuf_tensor("attn_wb", [P, 2, D], BF16))
        st6c = eC(nc.sbuf_tensor("st6c", [P, 12], F32))
        ebuf = eC(nc.sbuf_tensor("ebuf", [P, 2, KC, P], BF16))
        wglT_sb = eC(nc.sbuf_tensor("wglT_sb", [P, KC, D], BF16))
        mf_bf = eC(nc.sbuf_tensor("mf_bf", [1, D], BF16))
        mfT_bf = eC(nc.sbuf_tensor("mfT_bf", [P, KC], BF16))
        g_bf = eC(nc.sbuf_tensor("g_bf", [1, D], BF16))
        ps_q = eC(nc.psum_tensor("ps_q", [P, 3, 1024], F32))
        ps_pv2 = eC(nc.psum_tensor("ps_pv2", [P, 1024], F32))
        ptx = [dtp[0], dtp[1], dts[0], dts[1]]

        def pcb_gp(gp):
            gp.wait_ge(sv.h, max(sv_pa_done, sv_e_done))
            gp.wait_ge(st.h, pa_done_st)
            gp.wait_ge(ss.h, ss_pa_done)
            for s_, v_ in pa_tr_done:
                gp.wait_ge(s_.h, v_)
            gp.dma_start(out=wglT_sb[:],
                         in_=wglT_d.rearrange("(j p) d -> p j d", p=P)
                         ).then_inc(dwC.h, 16)
        block.gpsimd(pcb_gp)
        dwC.n += 16

        gstate = {"sv_gbc": None}

        def emit_b_piece(idx):
            if idx in (0, 1):
                nb = idx

                def fmean(t, nb=nb):
                    if nb == 0:
                        t.wait_ge(ss.h, ss_pa_done)
                        t.wait_ge(sv.h, max(sv_pa_done, sv_e_done, sv_ones))
                        for s_, v_ in pa_tr_done:
                            t.wait_ge(s_.h, v_)
                    for k in range(NT_ALL):
                        i = t.matmul(ps_pv2[0:1, nb * 512:(nb + 1) * 512],
                                     ones_col[:],
                                     s_tok[:, k, nb * 512:(nb + 1) * 512],
                                     start=(k == 0), stop=(k == NT_ALL - 1))
                    vinc(st, i)
                block.tensor(fmean)
                rec[("b_mean", nb)] = st.n
            elif idx == 2:
                def fv1(v):
                    v.wait_ge(st.h, rec[("b_mean", 1)])
                    i = v.tensor_scalar_mul(out=mf_bf[:], in0=ps_pv2[0:1, :],
                                            scalar1=1.0 / N)
                    vinc(sv, i)
                block.vector(fv1)
                rec[("b_mf",)] = sv.n

                def ft2(t):
                    t.wait_ge(sv.h, rec[("b_mf",)])
                    for j in range(KC):
                        i = t.matmul(ps_pv2[:, j:j + 1],
                                     mf_bf[0:1, j * P:(j + 1) * P],
                                     ones_col[0:1, 0:1], start=True, stop=True)
                    vinc(st, i)
                block.tensor(ft2)
                rec[("b_mfT",)] = st.n

                def fv2(v):
                    v.wait_ge(st.h, rec[("b_mfT",)])
                    i = v.tensor_copy(out=mfT_bf[:], in_=ps_pv2[:, 0:KC])
                    vinc(sv, i)
                block.vector(fv2)
                rec[("b_mfTe",)] = sv.n
            elif idx == 3:
                def ft3(t):
                    t.wait_ge(sv.h, rec[("b_mfTe",)])
                    t.wait_ge(dwC.h, dwC.n)
                    for nb in range(2):
                        for j in range(KC):
                            i = t.matmul(ps_pv2[0:1, nb * 512:(nb + 1) * 512],
                                         mfT_bf[:, j:j + 1],
                                         wglT_sb[:, j, nb * 512:(nb + 1) * 512],
                                         start=(j == 0), stop=(j == KC - 1))
                    vinc(st, i)
                block.tensor(ft3)
                rec[("b_g",)] = st.n

                def fv3(v):
                    v.wait_ge(st.h, rec[("b_g",)])
                    i = v.tensor_copy(out=g_bf[:], in_=ps_pv2[0:1, :])
                    vinc(sv, i)
                block.vector(fv3)
                rec[("b_gbf",)] = sv.n
            elif idx == 4:
                for nb in range(2):
                    def ftb(t, nb=nb):
                        t.wait_ge(sv.h, rec[("b_gbf",)] if nb == 0
                                  else rec[("b_gbc", 0)])
                        i = t.matmul(ps_pv2[:, 0:512], ones_row[:],
                                     g_bf[0:1, nb * 512:(nb + 1) * 512],
                                     start=True, stop=True)
                        vinc(st, i)
                    block.tensor(ftb)
                    rec[("b_gb", nb)] = st.n

                    def fvb(v, nb=nb):
                        v.wait_ge(st.h, rec[("b_gb", nb)])
                        i = v.tensor_copy(out=g_bc[:, nb * 512:(nb + 1) * 512],
                                          in_=ps_pv2[:, 0:512])
                        vinc(sv, i)
                    block.vector(fvb)
                    rec[("b_gbc", nb)] = sv.n
                gstate["sv_gbc"] = rec[("b_gbc", 1)]

        def e_load(qt):
            sem = deL[qt % 2]
            tbw = rec[("e_tb", qt // 4)]
            w_sc = rec.get(("scq", qt - 2, 3), 0)

            def f(sync, qt=qt, sem=sem, tbw=tbw, w_sc=w_sc):
                sync.wait_ge(deG[0].h, tbw[0])
                sync.wait_ge(deG[1].h, tbw[1])
                if qt <= 1:
                    sync.wait_ge(sv.h, max(sv_pa_done, sv_e_done))
                    sync.wait_ge(ss.h, ss_pa_done)
                    sync.wait_ge(st.h, pa_done_st)
                    for s_, v_ in pa_tr_done:
                        sync.wait_ge(s_.h, v_)
                    sync.wait_ge(deG[0].h, deG[0].n)
                    sync.wait_ge(deG[1].h, deG[1].n)
                if w_sc:
                    sync.wait_ge(st.h, w_sc)
                sync.dma_start(out=ebuf[:, qt % 2], in_=e_d[qt]
                               ).then_inc(sem.h, 16)
            block.sync(f)
            sem.n += 16
            rec[("e_ld", qt)] = (sem, sem.n)

        e_load(0)
        e_load(1)

        def c_scq(qt, qi):
            qg = 4 * qt + qi
            slot = qg % 3
            w_slot = rec.get(("expg", qg - 3), 0)

            def ftsq(t, qt=qt, qi=qi, slot=slot, w_slot=w_slot):
                if qi == 0:
                    w = rec[("e_ld", qt)]
                    t.wait_ge(w[0].h, w[1])
                    if qt == 0:
                        t.wait_ge(ss.h, ss_pa_done)
                        t.wait_ge(sv.h, max(sv_pa_done, sv_e_done))
                if w_slot:
                    t.wait_ge(ss.h, w_slot)
                for half in range(2):
                    for j in range(KC):
                        i = t.matmul(
                            ps_q[:, slot, half * 512:(half + 1) * 512],
                            ebuf[:, qt % 2, j, :],
                            s_FM[:, j, qi * 1024 + half * 512:
                                 qi * 1024 + (half + 1) * 512],
                            start=(j == 0), stop=(j == KC - 1))
                vinc(st, i)
            block.tensor(ftsq)
            rec[("scq", qt, qi)] = st.n

        def c_fvq(qt, qi):
            qg = 4 * qt + qi
            slot = qg % 3
            cp = qt % 2

            def fvq(v, qt=qt, qi=qi, slot=slot, cp=cp):
                v.wait_ge(st.h, rec[("scq", qt, qi)])
                if qi == 0 and qt >= 1:
                    v.wait_ge(ss.h, rec[("expq", qt - 1, 3)])
                i = v.reduce_max(out=qm4[:, cp, qi:qi + 1],
                                 in_=ps_q[:, slot, :], axis=AX.X)
                vinc(sv, i)
                v.wait_ge(sv.h, sv.n)
                i = v.tensor_scalar_mul(out=biasq[:, cp, qi:qi + 1],
                                        in0=qm4[:, cp, qi:qi + 1],
                                        scalar1=-INV32)
                vinc(sv, i)
            block.vector(fvq)
            rec[("qmaxb", qt, qi)] = sv.n

        def c_fexp(qt, qi):
            qg = 4 * qt + qi
            slot = qg % 3
            cp = qt % 2

            def fexp(s, qt=qt, qi=qi, slot=slot, cp=cp,
                     wv=rec[("qmaxb", qt, qi)],
                     wtr=rec.get(("ptr", qt - 1, qi), 0),
                     wrs=rec.get(("rsfix", qt - 1), 0)):
                s.wait_ge(sv.h, wv)
                if wtr:
                    s.wait_ge(wtr[0].h, wtr[1])
                if qi == 0 and wrs:
                    s.wait_ge(sv.h, wrs)
                s.activation(out=P_sb[:, qi * 1024:(qi + 1) * 1024],
                             in_=ps_q[:, slot, :], func=AF.Exp,
                             bias=biasq[:, cp, qi:qi + 1], scale=c_inv32[:],
                             accum_out=rsq[:, cp, qi:qi + 1]).then_inc(ss.h, 1)
            s_add(fexp)
            ss.n += 1
            rec[("expq", qt, qi)] = ss.n
            rec[("expg", 4 * qt + qi)] = ss.n

        def c_fptr(qt, qi):
            par = qt % 2
            sem = ptx[qi]
            w_pv = rec.get(("pvb", qt - 2, qi), 0)

            def fptr(s, qt=qt, qi=qi, par=par, sem=sem, w_pv=w_pv,
                     wss=rec[("expq", qt, qi)], w_self=sem.n):
                s.wait_ge(ss.h, wss)
                s.wait_ge(sem.h, w_self)
                if w_pv:
                    s.wait_ge(st.h, w_pv)
                s.dma_start_transpose(
                    out=PT_sb[:, par, qi * 1024:(qi + 1) * 1024]
                    .rearrange("p (j q) -> p j q", j=8),
                    in_=P_sb[:, qi * 1024:(qi + 1) * 1024]
                    ).then_inc(sem.h, 16)
            s_add(fptr)
            sem.n += 16
            rec[("ptr", qt, qi)] = (sem, sem.n)

        def c_quarter(qt, qi):
            c_scq(qt, qi)
            c_fvq(qt, qi)
            c_fexp(qt, qi)
            c_fptr(qt, qi)

        c_quarter(0, 0)
        for qt in range(NT_OWN):
            par = qt % 2
            cp = qt % 2
            for qi in (1, 2, 3):
                c_quarter(qt, qi)
                if qt == 0:
                    emit_b_piece(qi - 1)
            if qt == 0:
                emit_b_piece(3)

            if qt + 2 < NT_OWN:
                e_load(qt + 2)

            def fvc(v, qt=qt, cp=cp, wcf=rec.get(("cfac", qt - 2), 0)):
                if wcf:
                    v.wait_ge(ss.h, wcf)
                i = v.reduce_max(out=m_col[:, cp, :], in_=qm4[:, cp, :],
                                 axis=AX.X)
                vinc(sv, i)
                v.wait_ge(sv.h, sv.n)
                i = v.tensor_scalar(out=dcol[:, cp, :], in0=qm4[:, cp, :],
                                    scalar1=m_col[:, cp, :],
                                    scalar2=c_inv32[:],
                                    op0=ALU.subtract, op1=ALU.mult)
                vinc(sv, i)
            block.vector(fvc)
            rec[("dcol", qt)] = sv.n

            def fcf(s, qt=qt, cp=cp, wv=rec[("dcol", qt)]):
                s.wait_ge(sv.h, wv)
                s.activation(out=cfac[:, cp, :], in_=dcol[:, cp, :],
                             func=AF.Exp).then_inc(ss.h, 1)
            s_add(fcf)
            ss.n += 1
            rec[("cfac", qt)] = ss.n

            if qt + 1 < NT_OWN:
                c_scq(qt + 1, 0)

            for bq in range(4):
                def fvre(v, qt=qt, bq=bq, par=par, cp=cp):
                    w = rec[("ptr", qt, bq)]
                    v.wait_ge(w[0].h, w[1])
                    if bq == 0:
                        v.wait_ge(ss.h, rec[("cfac", qt)])
                    i = v.tensor_scalar(
                        out=PT_sb[:, par, bq * 1024:(bq + 1) * 1024],
                        in0=PT_sb[:, par, bq * 1024:(bq + 1) * 1024],
                        scalar1=cfac[:, cp, bq:bq + 1], scalar2=None,
                        op0=ALU.mult)
                    vinc(sv, i)
                block.vector(fvre)
                rec[("resc", qt, bq)] = sv.n

            def fvr(v, qt=qt, cp=cp):
                i = v.tensor_tensor(out=rsq[:, cp, :], in0=rsq[:, cp, :],
                                    in1=cfac[:, cp, :], op=ALU.mult)
                vinc(sv, i)
                v.wait_ge(sv.h, sv.n)
                i = v.reduce_sum(out=rstot[:, cp, :], in_=rsq[:, cp, :],
                                 axis=AX.X)
                vinc(sv, i)
                v.wait_ge(sv.h, sv.n)
                i = v.reciprocal(out=rinv[:, qt:qt + 1], in_=rstot[:, cp, :])
                vinc(sv, i)
            block.vector(fvr)
            rec[("rsfix", qt)] = sv.n

            if qt + 1 < NT_OWN:
                c_fvq(qt + 1, 0)
            if qt == 0:
                emit_b_piece(4)
            sv_gbc = gstate["sv_gbc"]

            for bq in range(4):
                def ftv(t, qt=qt, bq=bq, par=par, sv_gbc=sv_gbc):
                    t.wait_ge(sv.h, rec[("resc", qt, bq)])
                    if qt == 0 and bq == 0:
                        t.wait_ge(sv.h, sv_gbc)
                    for nb in range(2):
                        for cq in range(8):
                            cb = bq * 8 + cq
                            i = t.matmul(ps_pv2[:, nb * 512:(nb + 1) * 512],
                                         PT_sb[:, par, cb * P:(cb + 1) * P],
                                         s_tok[:, cb, nb * 512:(nb + 1) * 512],
                                         start=(cb == 0), stop=(cb == NT_ALL - 1))
                    vinc(st, i)
                block.tensor(ftv)
                rec[("pvb", qt, bq)] = st.n
            rec[("pv", qt)] = rec[("pvb", qt, 3)]

            if qt + 1 < NT_OWN:
                c_fexp(qt + 1, 0)
                c_fptr(qt + 1, 0)

            def fv4(v, qt=qt, par=par):
                v.wait_ge(st.h, rec[("pv", qt)])
                v.wait_ge(sv.h, rec[("rsfix", qt)])
                if qt == 0:
                    v.wait_ge(sv.h, sv_gbc)
                i = v.scalar_tensor_tensor(
                    out=tmp1[:], in0=ps_pv2[:], scalar=rinv[:, qt:qt + 1],
                    in1=g_bc[:], op0=ALU.mult, op1=ALU.add)
                vinc(sv, i)
                v.wait_ge(sv.h, sv.n)
                i = v.tensor_tensor(out=tmp2[:], in0=tmp1[:],
                                    in1=s_tok[:, qt, :], op=ALU.add)
                vinc(sv, i)
            block.vector(fv4)
            rec[("stt", qt)] = sv.n

            def fst(s, qt=qt, par=par, wv=rec[("stt", qt)],
                    wd=rec.get(("attn_dma", qt - 2), 0)):
                s.wait_ge(sv.h, wv)
                if qt == 0:
                    s.wait_ge(db.h, 32)
                if wd:
                    s.wait_ge(wd[0].h, wd[1])
                s.activation(out=attn_wb[:, par, :], in_=tmp2[:],
                             func=AF.Tanh, scale=beta_col[:]
                             ).then_inc(ss.h, 1)
            s_add(fst)
            ss.n += 1
            rec[("tanh", qt)] = ss.n

            def fv5(v, qt=qt, par=par):
                v.wait_ge(ss.h, rec[("tanh", qt)])
                v.bn_stats(out=st6c[:, 0:6], in_=attn_wb[:, par, 0:512])
                i = v.bn_stats(out=st6c[:, 6:12],
                               in_=attn_wb[:, par, 512:1024])
                vinc(sv, i)
                v.wait_ge(sv.h, sv.n)
                i = v.bn_aggr(out=mv2_all[:, qt, :], in_=st6c[:])
                vinc(sv, i)
            block.vector(fv5)
            rec[("bn2", qt)] = sv.n

            def fdma(s, qt=qt, par=par, wss=rec[("tanh", qt)]):
                s.wait_ge(ss.h, wss)
                s.dma_start(out=attn_d.rearrange("(q p) d -> q p d", p=P)[qt],
                            in_=attn_wb[:, par, :]).then_inc(dsm[qt % 2].h, 16)
            s_add(fdma)
            dsm[qt % 2].n += 16
            rec[("attn_dma", qt)] = (dsm[qt % 2], dsm[qt % 2].n)

        s_flush()
        pc_done = dict(st=rec[("pv", NT_OWN - 1)], sv=rec[("bn2", NT_OWN - 1)],
                       ss=rec[("tanh", NT_OWN - 1)],
                       tr=[(s_, s_.n) for s_ in (dtp[0], dtp[1], dts[0], dts[1])])
        ctxC.close()
        ctxA.close()

        ctx34 = ExitStack()
        h_act = ctx34.enter_context(
            nc.sbuf_tensor("h_act", [P, HC, T], BF16, side="right"))
        wg_sb = ctx34.enter_context(
            nc.sbuf_tensor("wg_sb", [P, 2, KC, P], BF16, side="right"))
        wu_sb = ctx34.enter_context(
            nc.sbuf_tensor("wu_sb", [P, 2, KC, P], BF16, side="right"))
        silu_b = ctx34.enter_context(
            nc.sbuf_tensor("silu_b", [P, 2, 512], F32, side="right"))
        ctxPS = ExitStack()
        ps_g = ctxPS.enter_context(nc.psum_tensor("ps_g", [P, 4, 512], F32))
        ps_u = ctxPS.enter_context(nc.psum_tensor("ps_u", [P, 4, 512], F32))

        ctx2 = ExitStack()
        e2 = ctx2.enter_context
        ctx23 = ExitStack()
        xn2T = ctx23.enter_context(nc.sbuf_tensor("xn2T", [P, KC, T], BF16))

        xs2 = e2(nc.sbuf_tensor("xs2", [P, 2, 2, D], F32))
        ab_buf = e2(nc.sbuf_tensor("ab_buf", [P, 2, 2, D], BF16))
        ln_tmp = e2(nc.sbuf_tensor("ln_tmp", [P, 2, D], F32))
        x1_buf = e2(nc.sbuf_tensor("x1_buf", [P, 2, D], F32))
        xn2_bf = e2(nc.sbuf_tensor("xn2_bf", [P, 2, D], BF16))
        sq2_all = e2(nc.sbuf_tensor("sq2_all", [P, NT_OWN], F32))
        rs2_all = e2(nc.sbuf_tensor("rs2_all", [P, NT_OWN], F32))
        st62 = e2(nc.sbuf_tensor("st62", [P, 2, 12], F32))
        mvb2 = e2(nc.sbuf_tensor("mvb2", [P, 2, 2], F32))
        sqv2 = e2(nc.sbuf_tensor("sqv2", [P, 2, 1], F32))
        rsig2 = e2(nc.sbuf_tensor("rsig2", [P, 2, 1], F32))

        def l2_s0(s):
            s.wait_ge(sv.h, pc_done["sv"])
            i = s.activation(out=sq2_all[:], in_=mv2_all[:, :, 1], func=AF.Sqrt,
                             bias=eps_col[:])
            vinc(ss, i)
        block.scalar(l2_s0)
        ss_sq2 = ss.n

        def l2_v0(v):
            v.wait_ge(ss.h, ss_sq2)
            i = v.reciprocal(out=rs2_all[:], in_=sq2_all[:])
            vinc(sv, i)
        block.vector(l2_v0)
        sv_l2s = sv.n

        attn_r = attn_d.rearrange("(c t p) d -> c p t d", t=2, p=P)
        x_own_r = x_d.rearrange("(c t p) d -> c p t d", t=2, p=P)

        def l2_chunk(c):
            sem = da[c % 2]
            w0 = rec[("attn_dma", 2 * c)]
            w1 = rec[("attn_dma", 2 * c + 1)]
            w2 = rec.get(("l2_lastread", c - 2), 0)
            first = (c <= 1)

            def f(sync, c=c, sem=sem, w0=w0, w1=w1, w2=w2, first=first):
                sync.wait_ge(w0[0].h, w0[1])
                sync.wait_ge(w1[0].h, w1[1])
                if first:
                    sync.wait_ge(st.h, pc_done["st"])
                    sync.wait_ge(sv.h, pc_done["sv"])
                    sync.wait_ge(ss.h, pc_done["ss"])
                    sync.wait_ge(deL[0].h, deL[0].n)
                    sync.wait_ge(deL[1].h, deL[1].n)
                    for s_, v_ in pc_done["tr"]:
                        sync.wait_ge(s_.h, v_)
                if w2:
                    sync.wait_ge(sv.h, w2)
                sync.dma_start(out=ab_buf[:, c % 2], in_=attn_r[c]
                               ).then_inc(sem.h, 16)
                sync.dma_start(out=xs2[:, c % 2], in_=x_own_r[c]
                               ).then_inc(sem.h, 16)
            block.sync(f)
            sem.n += 32
            rec[("l2chunk", c)] = (sem, sem.n)

        l2_chunk(0)
        l2_chunk(1)

        def l2_grpA(t_i):
            c, par = t_i // 2, t_i % 2
            w_sq = rec.get(("sq3", t_i - 2), 0)

            def fv1(v, t_i=t_i, c=c, par=par, w_sq=w_sq):
                w = rec[("l2chunk", c)]
                v.wait_ge(w[0].h, w[1])
                v.wait_ge(sv.h, sv_l2s)
                if w_sq:
                    v.wait_ge(ss.h, w_sq)
                i = v.tensor_scalar(out=ln_tmp[:, par, :],
                                    in0=ab_buf[:, c % 2, t_i % 2, :],
                                    scalar1=mv2_all[:, t_i, 0:1],
                                    scalar2=rs2_all[:, t_i:t_i + 1],
                                    op0=ALU.subtract, op1=ALU.mult)
                vinc(sv, i)
                v.wait_ge(sv.h, sv.n)
                if t_i >= 2:
                    w = rec[("x1_dma", t_i - 2)]
                    v.wait_ge(w[0].h, w[1])
                i = v.tensor_tensor(out=x1_buf[:, par, :], in0=ln_tmp[:, par, :],
                                    in1=xs2[:, c % 2, t_i % 2, :], op=ALU.add)
                vinc(sv, i)
                v.wait_ge(sv.h, sv.n)
                v.bn_stats(out=st62[:, par, 0:6], in_=x1_buf[:, par, 0:512])
                i = v.bn_stats(out=st62[:, par, 6:12],
                               in_=x1_buf[:, par, 512:1024])
                vinc(sv, i)
                v.wait_ge(sv.h, sv.n)
                i = v.bn_aggr(out=mvb2[:, par, :], in_=st62[:, par, :])
                vinc(sv, i)
            block.vector(fv1)
            rec[("x1", t_i)] = sv.n

            def fdma2(s, t_i=t_i, par=par, wv=sv.n):
                s.wait_ge(sv.h, wv)
                s.dma_start(out=x1_d.rearrange("(q p) d -> q p d", p=P)[t_i],
                            in_=x1_buf[:, par, :]).then_inc(dtm[t_i % 2].h, 16)
            s_add(fdma2)
            dtm[t_i % 2].n += 16
            rec[("x1_dma", t_i)] = (dtm[t_i % 2], dtm[t_i % 2].n)

            def fs2b(s, t_i=t_i, par=par, wv=sv.n):
                s.wait_ge(sv.h, wv)
                s.activation(out=sqv2[:, par, :], in_=mvb2[:, par, 1:2],
                             func=AF.Sqrt, scale=c_sc127[:], bias=c_b127[:]
                             ).then_inc(ss.h, 1)
            s_add(fs2b)
            ss.n += 1
            rec[("sq3", t_i)] = ss.n
            if t_i % 2 == 1:
                rec[("l2_lastread", c)] = rec[("x1", t_i)]
                if c + 2 < NT_OWN // 2:
                    l2_chunk(c + 2)

        def l2_grpB(t_i):
            par = t_i % 2

            def fv2b(v, t_i=t_i, par=par):
                v.wait_ge(ss.h, rec[("sq3", t_i)])
                i = v.reciprocal(out=rsig2[:, par, :], in_=sqv2[:, par, :])
                vinc(sv, i)
                v.wait_ge(sv.h, sv.n)
                if t_i >= 2:
                    w = rec[("xtr", t_i - 2)]
                    v.wait_ge(w[0].h, w[1])
                i = v.tensor_scalar(out=xn2_bf[:, par, :], in0=x1_buf[:, par, :],
                                    scalar1=mvb2[:, par, 0:1],
                                    scalar2=rsig2[:, par, :],
                                    op0=ALU.subtract, op1=ALU.mult)
                vinc(sv, i)
            block.vector(fv2b)
            rec[("xn2", t_i)] = sv.n

            sem_tr = dtp[t_i % 2]

            def fs3b(s, t_i=t_i, par=par, sem=sem_tr, wv=rec[("xn2", t_i)],
                     w_self=sem_tr.n):
                s.wait_ge(sv.h, wv)
                s.wait_ge(sem.h, w_self)
                s.dma_start_transpose(
                    out=xn2T[:, :, t_i * P:(t_i + 1) * P],
                    in_=xn2_bf[:, par, :]).then_inc(sem.h, 16)
            s_add(fs3b)
            sem_tr.n += 16
            rec[("xtr", t_i)] = (sem_tr, sem_tr.n)
            if t_i % 4 == 3:
                rec[("xtr4", t_i // 4)] = (dtp[0].n, dtp[1].n)

        def l2slot(t_i):
            if t_i < NT_OWN:
                l2_grpA(t_i)
            if t_i >= 1:
                l2_grpB(t_i - 1)

        for t_i in range(9):
            l2slot(t_i)


        wgT_r = wgT_d.rearrange("(j p) h -> p j h", p=P)
        wuT_r = wuT_d.rearrange("(j p) h -> p j h", p=P)
        wdT_r = wdT_d.rearrange("(c p) d -> p c d", p=P)
        last_silu = {}
        last_hmul_u = {}
        last_silu_buf = {}
        l2_done = {}

        def ffn_wload(ci):
            hb = ci % HC
            sem = dwP[ci % 2]
            w_war = rec.get(("up_ci", ci - 2), 0)
            first = (ci <= 1)

            def f(gp, hb=hb, sem=sem, par=ci % 2, w_war=w_war, first=first):
                if first:
                    gp.wait_ge(st.h, pc_done["st"])
                    gp.wait_ge(sv.h, pc_done["sv"])
                    gp.wait_ge(ss.h, pc_done["ss"])
                    for s_, v_ in pc_done["tr"]:
                        gp.wait_ge(s_.h, v_)
                    gp.wait_ge(deL[0].h, deL[0].n)
                    gp.wait_ge(deL[1].h, deL[1].n)
                    gp.wait_ge(dsm[0].h, dsm[0].n)
                    gp.wait_ge(dsm[1].h, dsm[1].n)
                if w_war:
                    gp.wait_ge(st.h, w_war)
                gp.dma_start(out=wg_sb[:, par],
                             in_=wgT_r[:, :, hb * P:(hb + 1) * P]
                             ).then_inc(sem.h, 16)
                gp.dma_start(out=wu_sb[:, par],
                             in_=wuT_r[:, :, hb * P:(hb + 1) * P]
                             ).then_inc(sem.h, 16)
            block.gpsimd(f)
            sem.n += 32
            rec[("wload_ci", ci)] = (sem, sem.n)

        def wdT_part(i4):
            def f(gp, i4=i4, w_self=dwF.n):
                gp.wait_ge(dwF.h, w_self)
                if i4 == 0:
                    gp.wait_ge(sv.h, l2_done["sv"])
                    gp.wait_ge(ss.h, l2_done["ss"])
                    gp.wait_ge(dtm[0].h, l2_done["dt0"])
                    gp.wait_ge(dtm[1].h, l2_done["dt1"])
                    gp.wait_ge(da[0].h, l2_done["da0"])
                    gp.wait_ge(da[1].h, l2_done["da1"])
                    for s_, v_ in l2_done["tr"]:
                        gp.wait_ge(s_.h, v_)
                gp.dma_start(out=wdT_sb[:, i4 * 6:(i4 + 1) * 6, :],
                             in_=wdT_r[:, i4 * 6:(i4 + 1) * 6, :]
                             ).then_inc(dwF.h, 16)
            block.gpsimd(f)
            dwF.n += 16

        def ffn_hb(ph, hb, tbs):
            ci = ph * HC + hb
            par = ci % 2
            for tb in tbs:
                def ftg(t, hb=hb, par=par, tb=tb, first=(tb == tbs[0]),
                        w4=rec[("xtr4", tb)], wl=rec[("wload_ci", ci)],
                        wsl=last_silu.get(tb, 0)):
                    if first:
                        t.wait_ge(wl[0].h, wl[1])
                    if hb == 0:
                        t.wait_ge(dtp[0].h, w4[0])
                        t.wait_ge(dtp[1].h, w4[1])
                    if wsl:
                        t.wait_ge(ss.h, wsl)
                    for j in range(KC):
                        i = t.matmul(ps_g[:, tb, :], wg_sb[:, par, j, :],
                                     xn2T[:, j, tb * 512:(tb + 1) * 512],
                                     start=(j == 0), stop=(j == KC - 1))
                    vinc(st, i)
                block.tensor(ftg)
                rec[("g_ci", ci, tb)] = st.n
            for tb in tbs:
                def ftu(t, hb=hb, par=par, tb=tb,
                        wuh=last_hmul_u.get(tb, 0)):
                    if wuh:
                        t.wait_ge(sv.h, wuh)
                    for j in range(KC):
                        i = t.matmul(ps_u[:, tb, :], wu_sb[:, par, j, :],
                                     xn2T[:, j, tb * 512:(tb + 1) * 512],
                                     start=(j == 0), stop=(j == KC - 1))
                    vinc(st, i)
                block.tensor(ftu)
                rec[("u_ci", ci, tb)] = st.n
            rec[("up_ci", ci)] = st.n
            for tb in tbs:
                def fsl(s, ci=ci, tb=tb, wt=rec[("g_ci", ci, tb)],
                        prev=last_silu_buf.get(tb % 2, 0)):
                    s.wait_ge(st.h, wt)
                    if prev:
                        s.wait_ge(sv.h, prev)
                    s.activation(out=silu_b[:, tb % 2, :], in_=ps_g[:, tb, :],
                                 func=AF.Silu).then_inc(ss.h, 1)
                s_add(fsl)
                ss.n += 1
                rec[("silu_ci", ci, tb)] = ss.n
                last_silu[tb] = ss.n

                def fvh(v, hb=hb, tb=tb, ci=ci,
                        wsl=rec[("silu_ci", ci, tb)],
                        wut=rec[("u_ci", ci, tb)]):
                    v.wait_ge(ss.h, wsl)
                    v.wait_ge(st.h, wut)
                    i = v.scalar_tensor_tensor(
                        out=h_act[:, hb, tb * 512:(tb + 1) * 512],
                        in0=silu_b[:, tb % 2, :], scalar=INV127,
                        in1=ps_u[:, tb, :], op0=ALU.mult, op1=ALU.mult)
                    vinc(sv, i)
                block.vector(fvh)
                rec[("hmul_ci", ci, tb)] = sv.n
                last_hmul_u[tb] = sv.n
                last_silu_buf[tb % 2] = sv.n

        ffn_wload(0)
        ffn_wload(1)
        for ph, tbs in ((0, (0, 1)), (1, (2, 3))):
            for hb in range(HC):
                ci = ph * HC + hb
                ffn_hb(ph, hb, tbs)
                if ci + 2 < 2 * HC:
                    ffn_wload(ci + 2)
                if ph == 0 and hb < 7:
                    l2slot(9 + hb)
                if ph == 0 and hb == 7:
                    l2slot(16)
                    l2_done.update(
                        sv=rec[("xn2", NT_OWN - 1)],
                        ss=rec[("sq3", NT_OWN - 1)],
                        dt0=dtm[0].n, dt1=dtm[1].n,
                        da0=da[0].n, da1=da[1].n,
                        tr=[(s_, s_.n) for s_ in (dtp[0], dtp[1])])
        hmul_done = rec[("hmul_ci", 2 * HC - 1, 3)]
        ffn_done = dict(st=rec[("up_ci", 2 * HC - 1)],
                        ss=rec[("silu_ci", 2 * HC - 1, 3)], sv=hmul_done)
        ctx2.close()
        ctx23.close()
        ctxPS.close()

        wdT_sb = ctx34.enter_context(
            nc.sbuf_tensor("wdT_sb", [P, HC, D], BF16, side="right"))
        for _i4 in range(4):
            wdT_part(_i4)
        ctx4 = ExitStack()
        e4 = ctx4.enter_context
        xs3 = e4(nc.sbuf_tensor("xs3", [P, 2, D], F32))
        ln4_b = e4(nc.sbuf_tensor("ln4_b", [P, 2, D], F32))
        out_b = e4(nc.sbuf_tensor("out_b", [P, 2, D], F32))
        st64 = e4(nc.sbuf_tensor("st64", [P, 2, 12], F32))
        mvb4 = e4(nc.sbuf_tensor("mvb4", [P, 2, 2], F32))
        sqv4 = e4(nc.sbuf_tensor("sqv4", [P, 2, 1], F32))
        rsig4 = e4(nc.sbuf_tensor("rsig4", [P, 2, 1], F32))
        ps_d = e4(nc.psum_tensor("ps_d", [P, 2, D], F32))

        x1_r = x1_d.rearrange("(q p) d -> q p d", p=P)

        def l4_x1(t_i):
            sem = da[t_i % 2]
            w1 = rec[("x1_dma", t_i)]
            w2 = rec.get(("outadd", t_i - 2), 0)
            first = (t_i <= 1)

            def f(sync, t_i=t_i, sem=sem, w1=w1, w2=w2, first=first):
                sync.wait_ge(w1[0].h, w1[1])
                if first:
                    sync.wait_ge(st.h, ffn_done["st"])
                    sync.wait_ge(ss.h, ffn_done["ss"])
                    sync.wait_ge(sv.h, ffn_done["sv"])
                if w2:
                    sync.wait_ge(sv.h, w2)
                sync.dma_start(out=xs3[:, t_i % 2, :], in_=x1_r[t_i]
                               ).then_inc(sem.h, 16)
            block.sync(f)
            sem.n += 16
            rec[("x1rd", t_i)] = (sem, sem.n)

        l4_x1(0)
        l4_x1(1)

        for t_i in range(NT_OWN):
            par = t_i % 2

            def ftd(t, t_i=t_i, par=par):
                if t_i == 0:
                    t.wait_ge(sv.h, hmul_done)
                if t_i >= 2:
                    t.wait_ge(sv.h, rec[("d_evac", t_i - 2)])
                for seg in range(4):
                    if t_i == 0:
                        t.wait_ge(dwF.h, 16 * (seg + 1))
                    for cb in range(seg * 6, (seg + 1) * 6):
                        for nb in range(2):
                            i = t.matmul(ps_d[:, par, nb * 512:(nb + 1) * 512],
                                         h_act[:, cb, t_i * P:(t_i + 1) * P],
                                         wdT_sb[:, cb, nb * 512:(nb + 1) * 512],
                                         start=(cb == 0), stop=(cb == HC - 1))
                vinc(st, i)
            block.tensor(ftd)
            rec[("d_mm", t_i)] = st.n

            def fvd(v, t_i=t_i, par=par):
                v.wait_ge(st.h, rec[("d_mm", t_i)])
                v.bn_stats(out=st64[:, par, 0:6], in_=ps_d[:, par, 0:512])
                i = v.bn_stats(out=st64[:, par, 6:12], in_=ps_d[:, par, 512:1024])
                vinc(sv, i)
                v.wait_ge(sv.h, sv.n)
                i = v.bn_aggr(out=mvb4[:, par, :], in_=st64[:, par, :])
                vinc(sv, i)
            block.vector(fvd)
            rec[("d_bn", t_i)] = sv.n

            def fsd(s, t_i=t_i, par=par, wv=rec[("d_bn", t_i)]):
                s.wait_ge(sv.h, wv)
                s.activation(out=sqv4[:, par, :], in_=mvb4[:, par, 1:2],
                             func=AF.Sqrt, bias=eps_col[:]).then_inc(ss.h, 1)
            s_add(fsd)
            ss.n += 1
            rec[("sq4", t_i)] = ss.n

            def fvd2(v, t_i=t_i, par=par):
                v.wait_ge(ss.h, rec[("sq4", t_i)])
                i = v.reciprocal(out=rsig4[:, par, :], in_=sqv4[:, par, :])
                vinc(sv, i)
                v.wait_ge(sv.h, sv.n)
                if t_i >= 2:
                    w = rec[("outdma", t_i - 2)]
                    v.wait_ge(w[0].h, w[1])
                i = v.tensor_scalar(out=ln4_b[:, par, :], in0=ps_d[:, par, :],
                                    scalar1=mvb4[:, par, 0:1],
                                    scalar2=rsig4[:, par, :],
                                    op0=ALU.subtract, op1=ALU.mult)
                vinc(sv, i)
                rec[("d_evac", t_i)] = sv.n
                v.wait_ge(sv.h, sv.n)
                w = rec[("x1rd", t_i)]
                v.wait_ge(w[0].h, w[1])
                i = v.tensor_tensor(out=out_b[:, par, :], in0=ln4_b[:, par, :],
                                    in1=xs3[:, par, :], op=ALU.add)
                vinc(sv, i)
            block.vector(fvd2)
            rec[("outadd", t_i)] = sv.n
            if t_i + 2 < NT_OWN:
                l4_x1(t_i + 2)

            def fod(sync, t_i=t_i, par=par):
                sem = do_[t_i % 2]
                sync.wait_ge(sv.h, rec[("outadd", t_i)])
                sync.dma_start(out=out_d.rearrange("(q p) d -> q p d", p=P)[t_i],
                               in_=out_b[:, par, :]).then_inc(sem.h, 16)
            block.sync(fod)
            do_[t_i % 2].n += 16
            rec[("outdma", t_i)] = (do_[t_i % 2], do_[t_i % 2].n)

        s_flush()
        ctx4.close()
        ctx34.close()

        all_dma = [da[0], da[1], db, do_[0], do_[1], dsm[0], dsm[1],
                   dtm[0], dtm[1], dwA, dwB, dwC, dwP[0], dwP[1], dwF,
                   dtp[0], dtp[1], dts[0], dts[1], deG[0], deG[1],
                   deL[0], deL[1]]

        def fin(sync):
            for s_ in all_dma:
                if s_.n:
                    sync.wait_ge(s_.h, s_.n)
        block.sync(fin)

    return nc


_NC_CACHE = None


def _get_nc():
    global _NC_CACHE
    if _NC_CACHE is None:
        _NC_CACHE = build_program()
    return _NC_CACHE


def kernel(L0_attn_x, L0_attn_W_spin, L0_attn_b_spin, L0_attn_W_global,
           L0_attn_W_J, L0_attn_beta, L0_ffn_w_gate, L0_ffn_w_up,
           L0_ffn_w_down):
    x = np.asarray(L0_attn_x, np.float32)
    wsp = np.asarray(L0_attn_W_spin, np.float32)
    bsp = np.asarray(L0_attn_b_spin, np.float32)
    wgl = np.asarray(L0_attn_W_global, np.float32)
    wj = np.asarray(L0_attn_W_J, np.float32)
    beta = np.asarray(L0_attn_beta, np.float32).reshape(-1)[0]
    wg = np.asarray(L0_ffn_w_gate, np.int32)
    wu = np.asarray(L0_ffn_w_up, np.int32)
    wd = np.asarray(L0_ffn_w_down, np.int32)

    wspT = np.ascontiguousarray(wsp.T)
    wjT = np.ascontiguousarray(wj.T)
    wglT = np.ascontiguousarray(wgl.T)
    wgT = np.ascontiguousarray(wg.T)
    wuT = np.ascontiguousarray(wu.T)
    wdT = np.ascontiguousarray(wd.T)
    b_bc = np.ascontiguousarray(np.broadcast_to(bsp[None, :], (P, D)))
    beta_col = np.full((P, 1), beta, np.float32)
    ident = np.eye(P, dtype=ml_dtypes.bfloat16)

    in_maps = []
    for c in range(8):
        b, h = c // 2, c % 2
        xp = np.ascontiguousarray(
            np.concatenate([x[b, h * T:(h + 1) * T],
                            x[b, (1 - h) * T:(2 - h) * T]], 0))
        in_maps.append({
            "x": xp, "wspT": wspT, "wjT": wjT, "wglT": wglT,
            "b_bc": b_bc, "beta_col": beta_col, "ident": ident,
            "wgT": wgT, "wuT": wuT, "wdT": wdT,
        })

    nc = _get_nc()
    res = run_bass_kernel_spmd(nc, in_maps, core_ids=list(range(8)))
    out = np.empty((4, N, D), np.float32)
    for c in range(8):
        b, h = c // 2, c % 2
        out[b, h * T:(h + 1) * T] = res.results[c]["out"]
    return out


# version 16
# speedup vs baseline: 1.0980x; 1.0980x over previous
import numpy as np
import ml_dtypes
from contextlib import ExitStack

import concourse.bass as bass
import concourse.mybir as mybir
from concourse.bass_utils import run_bass_kernel_spmd

F32 = mybir.dt.float32
BF16 = mybir.dt.bfloat16
I32 = mybir.dt.int32
AF = mybir.ActivationFunctionType
ALU = mybir.AluOpType
AX = mybir.AxisListType

P = 128
N = 4096
T = 2048
D = 1024
H = 3072
KC = D // P
HC = H // P
NT_ALL = N // P
NT_OWN = T // P
EPS = 1e-5
INV32 = 1.0 / 32.0
INV127 = 1.0 / 127.0


class Sem:
    def __init__(self, ctx, nc, name):
        self.h = ctx.enter_context(nc.semaphore(name))
        self.n = 0


def build_program():
    nc = bass.Bass("TRN2", target_bir_lowering=False)

    x_d = nc.dram_tensor("x", [N, D], F32, kind="ExternalInput").ap()
    wspT_d = nc.dram_tensor("wspT", [D, D], F32, kind="ExternalInput").ap()
    wjT_d = nc.dram_tensor("wjT", [D, D], F32, kind="ExternalInput").ap()
    wglT_d = nc.dram_tensor("wglT", [D, D], F32, kind="ExternalInput").ap()
    bbc_d = nc.dram_tensor("b_bc", [P, D], F32, kind="ExternalInput").ap()
    beta_d = nc.dram_tensor("beta_col", [P, 1], F32, kind="ExternalInput").ap()
    ident_d = nc.dram_tensor("ident", [P, P], BF16, kind="ExternalInput").ap()
    wgT_d = nc.dram_tensor("wgT", [D, H], I32, kind="ExternalInput").ap()
    wuT_d = nc.dram_tensor("wuT", [D, H], I32, kind="ExternalInput").ap()
    wdT_d = nc.dram_tensor("wdT", [H, D], I32, kind="ExternalInput").ap()
    out_d = nc.dram_tensor("out", [T, D], F32, kind="ExternalOutput").ap()
    attn_d = nc.dram_tensor("attn_sc", [T, D], BF16, kind="ExternalOutput").ap()
    x1_d = nc.dram_tensor("x1_sc", [T, D], F32, kind="ExternalOutput").ap()
    e_d = nc.dram_tensor("e_sc", [NT_OWN, P, KC, P], BF16,
                         kind="ExternalOutput").ap()

    ctx = ExitStack()
    with ctx:
        e = ctx.enter_context

        da = [Sem(ctx, nc, "da0"), Sem(ctx, nc, "da1")]
        db = Sem(ctx, nc, "db")
        do_ = [Sem(ctx, nc, "do0"), Sem(ctx, nc, "do1")]
        dsm = [Sem(ctx, nc, "ds0"), Sem(ctx, nc, "ds1")]
        dtm = [Sem(ctx, nc, "dt0"), Sem(ctx, nc, "dt1")]
        dwA = Sem(ctx, nc, "dwA")
        dwB = Sem(ctx, nc, "dwB")
        dwC = Sem(ctx, nc, "dwC")
        dwP = [Sem(ctx, nc, "dwD"), Sem(ctx, nc, "dwE")]
        dwF = Sem(ctx, nc, "dwF")
        dtp = [Sem(ctx, nc, "dtp0"), Sem(ctx, nc, "dtp1")]
        dts = [Sem(ctx, nc, "dts0"), Sem(ctx, nc, "dts1")]
        deG = [Sem(ctx, nc, "deG0"), Sem(ctx, nc, "deG1")]
        deL = [Sem(ctx, nc, "deL0"), Sem(ctx, nc, "deL1")]
        sv = Sem(ctx, nc, "sv")
        ss = Sem(ctx, nc, "ss")
        st = Sem(ctx, nc, "st")

        def vinc(sem, instr, k=1):
            instr.then_inc(sem.h, k)
            sem.n += k
            return sem.n

        ident = e(nc.sbuf_tensor("ident_sb", [P, P], BF16))
        beta_col = e(nc.sbuf_tensor("beta_sb", [P, 1], F32))
        ones_col = e(nc.sbuf_tensor("ones_col", [P, 1], BF16))
        ones_row = e(nc.sbuf_tensor("ones_row", [1, P], BF16))
        g_bc = e(nc.sbuf_tensor("g_bc", [P, D], F32))
        eps_col = e(nc.sbuf_tensor("eps_col", [P, 1], F32))
        c_inv32 = e(nc.sbuf_tensor("c_inv32", [P, 1], F32))
        c_sc127 = e(nc.sbuf_tensor("c_sc127", [P, 1], F32))
        c_b127 = e(nc.sbuf_tensor("c_b127", [P, 1], F32))
        mv2_all = e(nc.sbuf_tensor("mv2_all", [P, NT_OWN, 2], F32))

        block = e(nc.Block())
        rec = {}
        sdefer = []

        def s_add(fn):
            sdefer.append(fn)

        def s_flush():
            fns = list(sdefer)
            del sdefer[:]
            if fns:
                def run(s, fns=fns):
                    for f in fns:
                        f(s)
                block.scalar(run)

        def boot_vec(vector):
            vector.memset(ones_col[:], 1.0)
            vector.memset(ones_row[:], 1.0)
            vector.memset(eps_col[:], EPS)
            vector.memset(c_inv32[:], INV32)
            vector.memset(c_sc127[:], 127.0 * 127.0)
            m = vector.memset(c_b127[:], 127.0 * 127.0 * EPS)
            vinc(sv, m)

        ctxA = ExitStack()
        eA = ctxA.enter_context
        s_tok = eA(nc.sbuf_tensor("s_tok", [P, NT_ALL, D], BF16))
        s_FM = eA(nc.sbuf_tensor("s_FM", [P, KC, N], BF16))

        ctxA1 = ExitStack()
        e1 = ctxA1.enter_context
        wspT_sb = e1(nc.sbuf_tensor("wspT_sb", [P, KC, D], BF16))
        wjT_sb = e1(nc.sbuf_tensor("wjT_sb", [P, KC, D], BF16))
        e_stage = e1(nc.sbuf_tensor("e_stage", [P, 2, 512], BF16))
        b_row = e1(nc.sbuf_tensor("b_row", [1, D], BF16))
        xs = e1(nc.sbuf_tensor("xs", [P, 2, 2, D], F32))
        xn_bf = e1(nc.sbuf_tensor("xn_bf", [P, 2, D], BF16))
        xnT_sb = e1(nc.sbuf_tensor("xnT_sb", [P, 2, D], BF16))
        st6 = e1(nc.sbuf_tensor("st6", [P, 2, 12], F32))
        mvb = e1(nc.sbuf_tensor("mvb", [P, 2, 2], F32))
        sqv = e1(nc.sbuf_tensor("sqv", [P, 2, 1], F32))
        rsig = e1(nc.sbuf_tensor("rsig", [P, 2, 1], F32))
        ps_s = e1(nc.psum_tensor("ps_s", [P, 2, D], F32))
        ps_e = e1(nc.psum_tensor("ps_e", [P, 2, 512], F32))

        def boot_sync(sync):
            sync.dma_start(out=ident[:], in_=ident_d).then_inc(db.h, 16)
            sync.dma_start(out=beta_col[:], in_=beta_d).then_inc(db.h, 16)
        block.sync(boot_sync)
        db.n += 32
        block.vector(boot_vec)
        sv_ones = sv.n

        def boot_gp(gp):
            gp.dma_start(out=wspT_sb[:],
                         in_=wspT_d.rearrange("(j p) d -> p j d", p=P)
                         ).then_inc(dwA.h, 16)
            gp.dma_start(out=b_row[:], in_=bbc_d[0:1, :]).then_inc(dwA.h, 16)
            gp.dma_start(out=wjT_sb[:],
                         in_=wjT_d.rearrange("(j p) d -> p j d", p=P)
                         ).then_inc(dwB.h, 16)
        block.gpsimd(boot_gp)
        dwA.n += 32
        dwB.n += 16

        x_r = x_d.rearrange("(c t p) d -> c p t d", t=2, p=P)

        def emit_xchunk(c):
            sem = da[c % 2]
            w_ss = rec.get(("apply", 2 * c - 3), 0)

            def f(sync, c=c, sem=sem, w_ss=w_ss):
                if w_ss:
                    sync.wait_ge(sv.h, w_ss)
                sync.dma_start(out=xs[:, c % 2], in_=x_r[c]).then_inc(sem.h, 16)
            block.sync(f)
            sem.n += 16
            rec[("xchunk", c)] = (sem, sem.n)

        emit_xchunk(0)
        emit_xchunk(1)

        def pa_stats(k):
            c, par = k // 2, k % 2
            xsl = xs[:, c % 2, par, :]
            w_sq = rec.get(("sqrt", k - 2), 0)

            def fvA(v, k=k, c=c, par=par, xsl=xsl, w_sq=w_sq):
                w = rec[("xchunk", c)]
                v.wait_ge(w[0].h, w[1])
                if w_sq:
                    v.wait_ge(ss.h, w_sq)
                v.bn_stats(out=st6[:, par, 0:6], in_=xsl[:, 0:512])
                i = v.bn_stats(out=st6[:, par, 6:12], in_=xsl[:, 512:1024])
                vinc(sv, i)
                v.wait_ge(sv.h, sv.n)
                i = v.bn_aggr(out=mvb[:, par, :], in_=st6[:, par, :])
                vinc(sv, i)
            block.vector(fvA)
            rec[("statsA", k)] = sv.n

            def fsq(s, k=k, par=par, wA=sv.n):
                s.wait_ge(sv.h, wA)
                s.activation(out=sqv[:, par, :], in_=mvb[:, par, 1:2],
                             func=AF.Sqrt, bias=eps_col[:]).then_inc(ss.h, 1)
            s_add(fsq)
            ss.n += 1
            rec[("sqrt", k)] = ss.n

        def pa_apply(k):
            c, par = k // 2, k % 2
            xsl = xs[:, c % 2, par, :]
            w_tr = rec.get(("xntr", k - 2), 0)

            def fvB(v, k=k, par=par, xsl=xsl, w_tr=w_tr):
                v.wait_ge(ss.h, rec[("sqrt", k)])
                i = v.reciprocal(out=rsig[:, par, :], in_=sqv[:, par, :])
                vinc(sv, i)
                v.wait_ge(sv.h, sv.n)
                if w_tr:
                    v.wait_ge(w_tr[0].h, w_tr[1])
                i = v.tensor_scalar(out=xn_bf[:, par, :], in0=xsl,
                                    scalar1=mvb[:, par, 0:1],
                                    scalar2=rsig[:, par, :],
                                    op0=ALU.subtract, op1=ALU.mult)
                vinc(sv, i)
            block.vector(fvB)
            rec[("apply", k)] = sv.n

        def pa_xntr(k):
            par = k % 2
            sem = dtp[par]
            w_mm = rec.get(("mm", k - 2), 0)

            def fdt(s, k=k, par=par, sem=sem, wv=rec[("apply", k)], w_mm=w_mm):
                s.wait_ge(sv.h, wv)
                if w_mm:
                    s.wait_ge(st.h, w_mm)
                s.dma_start_transpose(
                    out=xnT_sb[:, par, :].rearrange("p (j q) -> p j q", j=KC),
                    in_=xn_bf[:, par, :]).then_inc(sem.h, 16)
            s_add(fdt)
            sem.n += 16
            rec[("xntr", k)] = (sem, sem.n)

        def pa_mm(k):
            par = k % 2
            w_bias = rec.get(("biasS", k - 2), 0)

            def ft2(t, k=k, par=par, w_bias=w_bias):
                w = rec[("xntr", k)]
                t.wait_ge(w[0].h, w[1])
                if k == 0:
                    t.wait_ge(dwA.h, 32)
                    t.wait_ge(sv.h, sv_ones)
                if w_bias:
                    t.wait_ge(ss.h, w_bias)
                for nb in range(2):
                    t.matmul(ps_s[:, par, nb * 512:(nb + 1) * 512],
                             ones_row[:], b_row[0:1, nb * 512:(nb + 1) * 512],
                             start=True, stop=False)
                    for j in range(KC):
                        i = t.matmul(ps_s[:, par, nb * 512:(nb + 1) * 512],
                                     xnT_sb[:, par, j * P:(j + 1) * P],
                                     wspT_sb[:, j, nb * 512:(nb + 1) * 512],
                                     start=False, stop=(j == KC - 1))
                vinc(st, i)
            block.tensor(ft2)
            rec[("mm", k)] = st.n

        def pa_bias(k):
            def fsb(s, k=k, par=k % 2, wt=rec[("mm", k)]):
                s.wait_ge(st.h, wt)
                s.copy(out=s_tok[:, k, :], in_=ps_s[:, par, :]).then_inc(ss.h, 1)
            s_add(fsb)
            ss.n += 1
            rec[("biasS", k)] = ss.n

        def pa_stra(k):
            sem = dts[k % 2]

            def fdt(s, k=k, sem=sem, w_self=sem.n):
                s.wait_ge(ss.h, rec[("biasS", k)])
                s.wait_ge(sem.h, w_self)
                s.dma_start_transpose(
                    out=s_FM[:, :, k * P:(k + 1) * P],
                    in_=s_tok[:, k, :]).then_inc(sem.h, 16)
            s_add(fdt)
            sem.n += 16
            rec[("stra", k)] = (sem, sem.n)
            if k % 4 == 3:
                rec[("stra4", k // 4)] = (dts[0].n, dts[1].n)

        e_sc_w = e_d.rearrange("n p j t -> p n j t")
        eq_groups = [(m, tb) for tb in range(4) for m in range(KC)]
        eg_state = {"i": 0}
        pending_evac = []

        def emit_energy_mm(gi):
            m, tb = eq_groups[gi]
            slot = gi % 2

            def fte(t, m=m, tb=tb, slot=slot, gi=gi):
                if gi == 0:
                    t.wait_ge(dwB.h, 16)
                w4 = rec[("stra4", tb)]
                t.wait_ge(dts[0].h, w4[0])
                t.wait_ge(dts[1].h, w4[1])
                if gi >= 2:
                    t.wait_ge(sv.h, rec[("e_evac", gi - 2)])
                for j in range(KC):
                    i = t.matmul(ps_e[:, slot, :],
                                 wjT_sb[:, j, m * P:(m + 1) * P],
                                 s_FM[:, j, tb * 512:(tb + 1) * 512],
                                 start=(j == 0), stop=(j == KC - 1))
                vinc(st, i)
            block.tensor(fte)
            rec[("e_mm", gi)] = st.n

        def emit_energy_evac(gi):
            m, tb = eq_groups[gi]
            slot = gi % 2
            w_wr = rec.get(("e_wr", gi - 2), 0)

            def fve(v, m=m, tb=tb, slot=slot, gi=gi, w_wr=w_wr):
                v.wait_ge(st.h, rec[("e_mm", gi)])
                if w_wr:
                    v.wait_ge(w_wr[0].h, w_wr[1])
                i = v.tensor_copy(out=e_stage[:, slot, :], in_=ps_e[:, slot, :])
                vinc(sv, i)
            block.vector(fve)
            rec[("e_evac", gi)] = sv.n

            sem = deG[gi % 2]

            def fgw(gp, m=m, tb=tb, slot=slot, gi=gi, sem=sem):
                gp.wait_ge(sv.h, rec[("e_evac", gi)])
                gp.dma_start(
                    out=e_sc_w[:, 4 * tb:4 * tb + 4, m, :],
                    in_=e_stage[:, slot, :].rearrange("p (a t) -> p a t", a=4)
                    ).then_inc(sem.h, 16)
            block.gpsimd(fgw)
            sem.n += 16
            rec[("e_wr", gi)] = (sem, sem.n)
            if m == KC - 1:
                rec[("e_tb", tb)] = (deG[0].n, deG[1].n)

        for k in range(NT_ALL + 2):
            if k < NT_ALL:
                pa_stats(k)
            if 1 <= k < NT_ALL + 1:
                pa_apply(k - 1)
                pa_xntr(k - 1)
            if 2 <= k:
                pa_bias(k - 2)
                pa_stra(k - 2)
            for gi in pending_evac:
                emit_energy_evac(gi)
            del pending_evac[:]
            if 1 <= k < NT_ALL + 1:
                pa_mm(k - 1)
                if (k - 1) % 2 == 1:
                    c2 = (k - 1) // 2 + 2
                    if c2 < NT_ALL // 2:
                        emit_xchunk(c2)
            while (eg_state["i"] < len(eq_groups)
                   and ("stra4", eq_groups[eg_state["i"]][1]) in rec
                   and len(pending_evac) < 2
                   and eg_state["i"] < max(0, (k - 4)) * 2):
                emit_energy_mm(eg_state["i"])
                pending_evac.append(eg_state["i"])
                eg_state["i"] += 1
        for gi in pending_evac:
            emit_energy_evac(gi)
        del pending_evac[:]
        while eg_state["i"] < len(eq_groups):
            emit_energy_mm(eg_state["i"])
            emit_energy_evac(eg_state["i"])
            eg_state["i"] += 1
        s_flush()
        ctxA1.close()
        sv_pa_done = rec[("apply", NT_ALL - 1)]
        ss_pa_done = rec[("biasS", NT_ALL - 1)]
        pa_tr_done = [(s_, s_.n) for s_ in (dtp[0], dtp[1], dts[0], dts[1])]
        pa_done_st = rec[("mm", NT_ALL - 1)]

        sv_e_done = rec[("e_evac", KC * 4 - 1)]

        ctxC = ExitStack()
        eC = ctxC.enter_context
        P_sb = eC(nc.sbuf_tensor("P_sb", [P, N], BF16))
        PT_sb = eC(nc.sbuf_tensor("PT_sb", [P, 2, N], BF16))
        qm4 = eC(nc.sbuf_tensor("qm4", [P, 2, 4], F32))
        m_col = eC(nc.sbuf_tensor("m_col", [P, 2, 1], F32))
        biasq = eC(nc.sbuf_tensor("biasq", [P, 2, 4], F32))
        dcol = eC(nc.sbuf_tensor("dcol", [P, 2, 4], F32))
        cfac = eC(nc.sbuf_tensor("cfac", [P, 2, 4], F32))
        rsq = eC(nc.sbuf_tensor("rsq", [P, 2, 4], F32))
        rstot = eC(nc.sbuf_tensor("rstot", [P, 2, 1], F32))
        rinv = eC(nc.sbuf_tensor("rinv", [P, NT_OWN], F32))
        tmp1 = eC(nc.sbuf_tensor("tmp1", [P, D], F32))
        tmp2 = eC(nc.sbuf_tensor("tmp2", [P, D], F32))
        attn_wb = eC(nc.sbuf_tensor("attn_wb", [P, 2, D], BF16))
        st6c = eC(nc.sbuf_tensor("st6c", [P, 12], F32))
        ebuf = eC(nc.sbuf_tensor("ebuf", [P, 2, KC, P], BF16))
        wglT_sb = eC(nc.sbuf_tensor("wglT_sb", [P, KC, D], BF16))
        mf_bf = eC(nc.sbuf_tensor("mf_bf", [1, D], BF16))
        mfT_bf = eC(nc.sbuf_tensor("mfT_bf", [P, KC], BF16))
        g_bf = eC(nc.sbuf_tensor("g_bf", [1, D], BF16))
        ps_q = eC(nc.psum_tensor("ps_q", [P, 3, 1024], F32))
        ps_pv2 = eC(nc.psum_tensor("ps_pv2", [P, 1024], F32))
        ptx = [dtp[0], dtp[1], dts[0], dts[1]]

        def pcb_gp(gp):
            gp.wait_ge(sv.h, max(sv_pa_done, sv_e_done))
            gp.wait_ge(st.h, pa_done_st)
            gp.wait_ge(ss.h, ss_pa_done)
            for s_, v_ in pa_tr_done:
                gp.wait_ge(s_.h, v_)
            gp.dma_start(out=wglT_sb[:],
                         in_=wglT_d.rearrange("(j p) d -> p j d", p=P)
                         ).then_inc(dwC.h, 16)
        block.gpsimd(pcb_gp)
        dwC.n += 16

        gstate = {"sv_gbc": None}

        def emit_b_piece(idx):
            if idx in (0, 1):
                nb = idx

                def fmean(t, nb=nb):
                    if nb == 0:
                        t.wait_ge(ss.h, ss_pa_done)
                        t.wait_ge(sv.h, max(sv_pa_done, sv_e_done, sv_ones))
                        for s_, v_ in pa_tr_done:
                            t.wait_ge(s_.h, v_)
                    for k in range(NT_ALL):
                        i = t.matmul(ps_pv2[0:1, nb * 512:(nb + 1) * 512],
                                     ones_col[:],
                                     s_tok[:, k, nb * 512:(nb + 1) * 512],
                                     start=(k == 0), stop=(k == NT_ALL - 1))
                    vinc(st, i)
                block.tensor(fmean)
                rec[("b_mean", nb)] = st.n
            elif idx == 2:
                def fv1(v):
                    v.wait_ge(st.h, rec[("b_mean", 1)])
                    i = v.tensor_scalar_mul(out=mf_bf[:], in0=ps_pv2[0:1, :],
                                            scalar1=1.0 / N)
                    vinc(sv, i)
                block.vector(fv1)
                rec[("b_mf",)] = sv.n

                def ft2(t):
                    t.wait_ge(sv.h, rec[("b_mf",)])
                    for j in range(KC):
                        i = t.matmul(ps_pv2[:, j:j + 1],
                                     mf_bf[0:1, j * P:(j + 1) * P],
                                     ones_col[0:1, 0:1], start=True, stop=True)
                    vinc(st, i)
                block.tensor(ft2)
                rec[("b_mfT",)] = st.n

                def fv2(v):
                    v.wait_ge(st.h, rec[("b_mfT",)])
                    i = v.tensor_copy(out=mfT_bf[:], in_=ps_pv2[:, 0:KC])
                    vinc(sv, i)
                block.vector(fv2)
                rec[("b_mfTe",)] = sv.n
            elif idx == 3:
                def ft3(t):
                    t.wait_ge(sv.h, rec[("b_mfTe",)])
                    t.wait_ge(dwC.h, dwC.n)
                    for nb in range(2):
                        for j in range(KC):
                            i = t.matmul(ps_pv2[0:1, nb * 512:(nb + 1) * 512],
                                         mfT_bf[:, j:j + 1],
                                         wglT_sb[:, j, nb * 512:(nb + 1) * 512],
                                         start=(j == 0), stop=(j == KC - 1))
                    vinc(st, i)
                block.tensor(ft3)
                rec[("b_g",)] = st.n

                def fv3(v):
                    v.wait_ge(st.h, rec[("b_g",)])
                    i = v.tensor_copy(out=g_bf[:], in_=ps_pv2[0:1, :])
                    vinc(sv, i)
                block.vector(fv3)
                rec[("b_gbf",)] = sv.n
            elif idx == 4:
                for nb in range(2):
                    def ftb(t, nb=nb):
                        t.wait_ge(sv.h, rec[("b_gbf",)] if nb == 0
                                  else rec[("b_gbc", 0)])
                        i = t.matmul(ps_pv2[:, 0:512], ones_row[:],
                                     g_bf[0:1, nb * 512:(nb + 1) * 512],
                                     start=True, stop=True)
                        vinc(st, i)
                    block.tensor(ftb)
                    rec[("b_gb", nb)] = st.n

                    def fvb(v, nb=nb):
                        v.wait_ge(st.h, rec[("b_gb", nb)])
                        i = v.tensor_copy(out=g_bc[:, nb * 512:(nb + 1) * 512],
                                          in_=ps_pv2[:, 0:512])
                        vinc(sv, i)
                    block.vector(fvb)
                    rec[("b_gbc", nb)] = sv.n
                gstate["sv_gbc"] = rec[("b_gbc", 1)]

        def e_load(qt):
            sem = deL[qt % 2]
            tbw = rec[("e_tb", qt // 4)]
            w_sc = rec.get(("scq", qt - 2, 3), 0)

            def f(sync, qt=qt, sem=sem, tbw=tbw, w_sc=w_sc):
                sync.wait_ge(deG[0].h, tbw[0])
                sync.wait_ge(deG[1].h, tbw[1])
                if qt <= 1:
                    sync.wait_ge(sv.h, max(sv_pa_done, sv_e_done))
                    sync.wait_ge(ss.h, ss_pa_done)
                    sync.wait_ge(st.h, pa_done_st)
                    for s_, v_ in pa_tr_done:
                        sync.wait_ge(s_.h, v_)
                    sync.wait_ge(deG[0].h, deG[0].n)
                    sync.wait_ge(deG[1].h, deG[1].n)
                if w_sc:
                    sync.wait_ge(st.h, w_sc)
                sync.dma_start(out=ebuf[:, qt % 2], in_=e_d[qt]
                               ).then_inc(sem.h, 16)
            block.sync(f)
            sem.n += 16
            rec[("e_ld", qt)] = (sem, sem.n)

        e_load(0)
        e_load(1)

        def c_scq(qt, qi):
            qg = 4 * qt + qi
            slot = qg % 3
            w_slot = rec.get(("expg", qg - 3), 0)

            def ftsq(t, qt=qt, qi=qi, slot=slot, w_slot=w_slot):
                if qi == 0:
                    w = rec[("e_ld", qt)]
                    t.wait_ge(w[0].h, w[1])
                    if qt == 0:
                        t.wait_ge(ss.h, ss_pa_done)
                        t.wait_ge(sv.h, max(sv_pa_done, sv_e_done))
                if w_slot:
                    t.wait_ge(ss.h, w_slot)
                for half in range(2):
                    for j in range(KC):
                        i = t.matmul(
                            ps_q[:, slot, half * 512:(half + 1) * 512],
                            ebuf[:, qt % 2, j, :],
                            s_FM[:, j, qi * 1024 + half * 512:
                                 qi * 1024 + (half + 1) * 512],
                            start=(j == 0), stop=(j == KC - 1))
                vinc(st, i)
            block.tensor(ftsq)
            rec[("scq", qt, qi)] = st.n

        def c_fvq(qt, qi):
            qg = 4 * qt + qi
            slot = qg % 3
            cp = qt % 2

            def fvq(v, qt=qt, qi=qi, slot=slot, cp=cp):
                v.wait_ge(st.h, rec[("scq", qt, qi)])
                if qi == 0 and qt >= 1:
                    v.wait_ge(ss.h, rec[("expq", qt - 1, 3)])
                i = v.reduce_max(out=qm4[:, cp, qi:qi + 1],
                                 in_=ps_q[:, slot, :], axis=AX.X)
                vinc(sv, i)
                v.wait_ge(sv.h, sv.n)
                i = v.tensor_scalar_mul(out=biasq[:, cp, qi:qi + 1],
                                        in0=qm4[:, cp, qi:qi + 1],
                                        scalar1=-INV32)
                vinc(sv, i)
            block.vector(fvq)
            rec[("qmaxb", qt, qi)] = sv.n

        def c_fexp(qt, qi):
            qg = 4 * qt + qi
            slot = qg % 3
            cp = qt % 2

            def fexp(s, qt=qt, qi=qi, slot=slot, cp=cp,
                     wv=rec[("qmaxb", qt, qi)],
                     wtr=rec.get(("ptr", qt - 1, qi), 0),
                     wrs=rec.get(("rsfix", qt - 1), 0)):
                s.wait_ge(sv.h, wv)
                if wtr:
                    s.wait_ge(wtr[0].h, wtr[1])
                if qi == 0 and wrs:
                    s.wait_ge(sv.h, wrs)
                s.activation(out=P_sb[:, qi * 1024:(qi + 1) * 1024],
                             in_=ps_q[:, slot, :], func=AF.Exp,
                             bias=biasq[:, cp, qi:qi + 1], scale=c_inv32[:],
                             accum_out=rsq[:, cp, qi:qi + 1]).then_inc(ss.h, 1)
            s_add(fexp)
            ss.n += 1
            rec[("expq", qt, qi)] = ss.n
            rec[("expg", 4 * qt + qi)] = ss.n

        def c_fptr(qt, qi):
            par = qt % 2
            sem = ptx[qi]
            w_pv = rec.get(("pvb", qt - 2, qi), 0)

            def fptr(sy, qt=qt, qi=qi, par=par, sem=sem, w_pv=w_pv,
                     wss=rec[("expq", qt, qi)], w_self=sem.n):
                sy.wait_ge(ss.h, wss)
                sy.wait_ge(sem.h, w_self)
                if w_pv:
                    sy.wait_ge(st.h, w_pv)
                sy.dma_start_transpose(
                    out=PT_sb[:, par, qi * 1024:(qi + 1) * 1024]
                    .rearrange("p (j q) -> p j q", j=8),
                    in_=P_sb[:, qi * 1024:(qi + 1) * 1024]
                    ).then_inc(sem.h, 16)
            block.sync(fptr)
            sem.n += 16
            rec[("ptr", qt, qi)] = (sem, sem.n)

        def c_quarter(qt, qi):
            c_scq(qt, qi)
            c_fvq(qt, qi)
            c_fexp(qt, qi)
            c_fptr(qt, qi)

        pend_bn2 = []
        c_quarter(0, 0)
        for qt in range(NT_OWN):
            if pend_bn2 and qt >= 1:
                for qt5, f5 in pend_bn2:
                    block.vector(f5)
                    rec[("bn2", qt5)] = sv.n
                del pend_bn2[:]
            par = qt % 2
            cp = qt % 2
            for qi in (1, 2, 3):
                c_quarter(qt, qi)
                if qt == 0:
                    emit_b_piece(qi - 1)
            if qt == 0:
                emit_b_piece(3)

            if qt + 2 < NT_OWN:
                e_load(qt + 2)

            def fvc(v, qt=qt, cp=cp, wcf=rec.get(("cfac", qt - 2), 0)):
                if wcf:
                    v.wait_ge(ss.h, wcf)
                i = v.reduce_max(out=m_col[:, cp, :], in_=qm4[:, cp, :],
                                 axis=AX.X)
                vinc(sv, i)
                v.wait_ge(sv.h, sv.n)
                i = v.tensor_scalar(out=dcol[:, cp, :], in0=qm4[:, cp, :],
                                    scalar1=m_col[:, cp, :],
                                    scalar2=c_inv32[:],
                                    op0=ALU.subtract, op1=ALU.mult)
                vinc(sv, i)
            block.vector(fvc)
            rec[("dcol", qt)] = sv.n

            def fcf(s, qt=qt, cp=cp, wv=rec[("dcol", qt)]):
                s.wait_ge(sv.h, wv)
                s.activation(out=cfac[:, cp, :], in_=dcol[:, cp, :],
                             func=AF.Exp).then_inc(ss.h, 1)
            s_add(fcf)
            ss.n += 1
            rec[("cfac", qt)] = ss.n

            if qt + 1 < NT_OWN:
                c_scq(qt + 1, 0)

            for bq in range(4):
                def fvre(v, qt=qt, bq=bq, par=par, cp=cp):
                    w = rec[("ptr", qt, bq)]
                    v.wait_ge(w[0].h, w[1])
                    if bq == 0:
                        v.wait_ge(ss.h, rec[("cfac", qt)])
                    i = v.tensor_scalar(
                        out=PT_sb[:, par, bq * 1024:(bq + 1) * 1024],
                        in0=PT_sb[:, par, bq * 1024:(bq + 1) * 1024],
                        scalar1=cfac[:, cp, bq:bq + 1], scalar2=None,
                        op0=ALU.mult)
                    vinc(sv, i)
                block.vector(fvre)
                rec[("resc", qt, bq)] = sv.n

            def fvr(v, qt=qt, cp=cp):
                i = v.tensor_tensor(out=rsq[:, cp, :], in0=rsq[:, cp, :],
                                    in1=cfac[:, cp, :], op=ALU.mult)
                vinc(sv, i)
                v.wait_ge(sv.h, sv.n)
                i = v.reduce_sum(out=rstot[:, cp, :], in_=rsq[:, cp, :],
                                 axis=AX.X)
                vinc(sv, i)
                v.wait_ge(sv.h, sv.n)
                i = v.reciprocal(out=rinv[:, qt:qt + 1], in_=rstot[:, cp, :])
                vinc(sv, i)
            block.vector(fvr)
            rec[("rsfix", qt)] = sv.n

            if qt + 1 < NT_OWN:
                c_fvq(qt + 1, 0)
            if qt == 0:
                emit_b_piece(4)
            sv_gbc = gstate["sv_gbc"]

            for bq in range(4):
                def ftv(t, qt=qt, bq=bq, par=par, sv_gbc=sv_gbc):
                    t.wait_ge(sv.h, rec[("resc", qt, bq)])
                    if qt == 0 and bq == 0:
                        t.wait_ge(sv.h, sv_gbc)
                    for nb in range(2):
                        for cq in range(8):
                            cb = bq * 8 + cq
                            i = t.matmul(ps_pv2[:, nb * 512:(nb + 1) * 512],
                                         PT_sb[:, par, cb * P:(cb + 1) * P],
                                         s_tok[:, cb, nb * 512:(nb + 1) * 512],
                                         start=(cb == 0), stop=(cb == NT_ALL - 1))
                    vinc(st, i)
                block.tensor(ftv)
                rec[("pvb", qt, bq)] = st.n
            rec[("pv", qt)] = rec[("pvb", qt, 3)]

            if qt + 1 < NT_OWN:
                c_fexp(qt + 1, 0)
                c_fptr(qt + 1, 0)

            def fv4(v, qt=qt, par=par):
                v.wait_ge(st.h, rec[("pv", qt)])
                v.wait_ge(sv.h, rec[("rsfix", qt)])
                if qt == 0:
                    v.wait_ge(sv.h, sv_gbc)
                i = v.scalar_tensor_tensor(
                    out=tmp1[:], in0=ps_pv2[:], scalar=rinv[:, qt:qt + 1],
                    in1=g_bc[:], op0=ALU.mult, op1=ALU.add)
                vinc(sv, i)
                v.wait_ge(sv.h, sv.n)
                i = v.tensor_tensor(out=tmp2[:], in0=tmp1[:],
                                    in1=s_tok[:, qt, :], op=ALU.add)
                vinc(sv, i)
            block.vector(fv4)
            rec[("stt", qt)] = sv.n

            def fst(s, qt=qt, par=par, wv=rec[("stt", qt)],
                    wd=rec.get(("attn_dma", qt - 2), 0)):
                s.wait_ge(sv.h, wv)
                if qt == 0:
                    s.wait_ge(db.h, 32)
                if wd:
                    s.wait_ge(wd[0].h, wd[1])
                s.activation(out=attn_wb[:, par, :], in_=tmp2[:],
                             func=AF.Tanh, scale=beta_col[:]
                             ).then_inc(ss.h, 1)
            s_add(fst)
            ss.n += 1
            rec[("tanh", qt)] = ss.n

            def fv5(v, qt=qt, par=par):
                v.wait_ge(ss.h, rec[("tanh", qt)])
                v.bn_stats(out=st6c[:, 0:6], in_=attn_wb[:, par, 0:512])
                i = v.bn_stats(out=st6c[:, 6:12],
                               in_=attn_wb[:, par, 512:1024])
                vinc(sv, i)
                v.wait_ge(sv.h, sv.n)
                i = v.bn_aggr(out=mv2_all[:, qt, :], in_=st6c[:])
                vinc(sv, i)
            pend_bn2.append((qt, fv5))
            if qt == NT_OWN - 1:
                for qt5, f5 in pend_bn2:
                    block.vector(f5)
                    rec[("bn2", qt5)] = sv.n
                del pend_bn2[:]

            def fdma(s, qt=qt, par=par, wss=rec[("tanh", qt)]):
                s.wait_ge(ss.h, wss)
                s.dma_start(out=attn_d.rearrange("(q p) d -> q p d", p=P)[qt],
                            in_=attn_wb[:, par, :]).then_inc(dsm[qt % 2].h, 16)
            s_add(fdma)
            dsm[qt % 2].n += 16
            rec[("attn_dma", qt)] = (dsm[qt % 2], dsm[qt % 2].n)

        s_flush()
        pc_done = dict(st=rec[("pv", NT_OWN - 1)], sv=rec[("bn2", NT_OWN - 1)],
                       ss=rec[("tanh", NT_OWN - 1)],
                       tr=[(s_, s_.n) for s_ in (dtp[0], dtp[1], dts[0], dts[1])])
        ctxC.close()
        ctxA.close()

        ctx34 = ExitStack()
        h_act = ctx34.enter_context(
            nc.sbuf_tensor("h_act", [P, HC, T], BF16, side="right"))
        wg_sb = ctx34.enter_context(
            nc.sbuf_tensor("wg_sb", [P, 2, KC, P], BF16, side="right"))
        wu_sb = ctx34.enter_context(
            nc.sbuf_tensor("wu_sb", [P, 2, KC, P], BF16, side="right"))
        silu_b = ctx34.enter_context(
            nc.sbuf_tensor("silu_b", [P, 2, 512], F32, side="right"))
        ctxPS = ExitStack()
        ps_g = ctxPS.enter_context(nc.psum_tensor("ps_g", [P, 4, 512], F32))
        ps_u = ctxPS.enter_context(nc.psum_tensor("ps_u", [P, 4, 512], F32))

        ctx2 = ExitStack()
        e2 = ctx2.enter_context
        ctx23 = ExitStack()
        xn2T = ctx23.enter_context(nc.sbuf_tensor("xn2T", [P, KC, T], BF16))

        xs2 = e2(nc.sbuf_tensor("xs2", [P, 2, 2, D], F32))
        ab_buf = e2(nc.sbuf_tensor("ab_buf", [P, 2, 2, D], BF16))
        ln_tmp = e2(nc.sbuf_tensor("ln_tmp", [P, 2, D], F32))
        x1_buf = e2(nc.sbuf_tensor("x1_buf", [P, 2, D], F32))
        xn2_bf = e2(nc.sbuf_tensor("xn2_bf", [P, 2, D], BF16))
        sq2_all = e2(nc.sbuf_tensor("sq2_all", [P, NT_OWN], F32))
        rs2_all = e2(nc.sbuf_tensor("rs2_all", [P, NT_OWN], F32))
        st62 = e2(nc.sbuf_tensor("st62", [P, 2, 12], F32))
        mvb2 = e2(nc.sbuf_tensor("mvb2", [P, 2, 2], F32))
        sqv2 = e2(nc.sbuf_tensor("sqv2", [P, 2, 1], F32))
        rsig2 = e2(nc.sbuf_tensor("rsig2", [P, 2, 1], F32))

        def l2_s0(s):
            s.wait_ge(sv.h, pc_done["sv"])
            i = s.activation(out=sq2_all[:], in_=mv2_all[:, :, 1], func=AF.Sqrt,
                             bias=eps_col[:])
            vinc(ss, i)
        block.scalar(l2_s0)
        ss_sq2 = ss.n

        def l2_v0(v):
            v.wait_ge(ss.h, ss_sq2)
            i = v.reciprocal(out=rs2_all[:], in_=sq2_all[:])
            vinc(sv, i)
        block.vector(l2_v0)
        sv_l2s = sv.n

        attn_r = attn_d.rearrange("(c t p) d -> c p t d", t=2, p=P)
        x_own_r = x_d.rearrange("(c t p) d -> c p t d", t=2, p=P)

        def l2_chunk(c):
            sem = da[c % 2]
            w0 = rec[("attn_dma", 2 * c)]
            w1 = rec[("attn_dma", 2 * c + 1)]
            w2 = rec.get(("l2_lastread", c - 2), 0)
            first = (c <= 1)

            def f(sync, c=c, sem=sem, w0=w0, w1=w1, w2=w2, first=first):
                sync.wait_ge(w0[0].h, w0[1])
                sync.wait_ge(w1[0].h, w1[1])
                if first:
                    sync.wait_ge(st.h, pc_done["st"])
                    sync.wait_ge(sv.h, pc_done["sv"])
                    sync.wait_ge(ss.h, pc_done["ss"])
                    sync.wait_ge(deL[0].h, deL[0].n)
                    sync.wait_ge(deL[1].h, deL[1].n)
                    for s_, v_ in pc_done["tr"]:
                        sync.wait_ge(s_.h, v_)
                if w2:
                    sync.wait_ge(sv.h, w2)
                sync.dma_start(out=ab_buf[:, c % 2], in_=attn_r[c]
                               ).then_inc(sem.h, 16)
                sync.dma_start(out=xs2[:, c % 2], in_=x_own_r[c]
                               ).then_inc(sem.h, 16)
            block.sync(f)
            sem.n += 32
            rec[("l2chunk", c)] = (sem, sem.n)

        l2_chunk(0)
        l2_chunk(1)

        def l2_grpA(t_i):
            c, par = t_i // 2, t_i % 2
            w_sq = rec.get(("sq3", t_i - 2), 0)

            def fv1(v, t_i=t_i, c=c, par=par, w_sq=w_sq):
                w = rec[("l2chunk", c)]
                v.wait_ge(w[0].h, w[1])
                v.wait_ge(sv.h, sv_l2s)
                if w_sq:
                    v.wait_ge(ss.h, w_sq)
                i = v.tensor_scalar(out=ln_tmp[:, par, :],
                                    in0=ab_buf[:, c % 2, t_i % 2, :],
                                    scalar1=mv2_all[:, t_i, 0:1],
                                    scalar2=rs2_all[:, t_i:t_i + 1],
                                    op0=ALU.subtract, op1=ALU.mult)
                vinc(sv, i)
                v.wait_ge(sv.h, sv.n)
                if t_i >= 2:
                    w = rec[("x1_dma", t_i - 2)]
                    v.wait_ge(w[0].h, w[1])
                i = v.tensor_tensor(out=x1_buf[:, par, :], in0=ln_tmp[:, par, :],
                                    in1=xs2[:, c % 2, t_i % 2, :], op=ALU.add)
                vinc(sv, i)
                v.wait_ge(sv.h, sv.n)
                v.bn_stats(out=st62[:, par, 0:6], in_=x1_buf[:, par, 0:512])
                i = v.bn_stats(out=st62[:, par, 6:12],
                               in_=x1_buf[:, par, 512:1024])
                vinc(sv, i)
                v.wait_ge(sv.h, sv.n)
                i = v.bn_aggr(out=mvb2[:, par, :], in_=st62[:, par, :])
                vinc(sv, i)
            block.vector(fv1)
            rec[("x1", t_i)] = sv.n

            def fdma2(s, t_i=t_i, par=par, wv=sv.n):
                s.wait_ge(sv.h, wv)
                s.dma_start(out=x1_d.rearrange("(q p) d -> q p d", p=P)[t_i],
                            in_=x1_buf[:, par, :]).then_inc(dtm[t_i % 2].h, 16)
            s_add(fdma2)
            dtm[t_i % 2].n += 16
            rec[("x1_dma", t_i)] = (dtm[t_i % 2], dtm[t_i % 2].n)

            def fs2b(s, t_i=t_i, par=par, wv=sv.n):
                s.wait_ge(sv.h, wv)
                s.activation(out=sqv2[:, par, :], in_=mvb2[:, par, 1:2],
                             func=AF.Sqrt, scale=c_sc127[:], bias=c_b127[:]
                             ).then_inc(ss.h, 1)
            s_add(fs2b)
            ss.n += 1
            rec[("sq3", t_i)] = ss.n
            if t_i % 2 == 1:
                rec[("l2_lastread", c)] = rec[("x1", t_i)]
                if c + 2 < NT_OWN // 2:
                    l2_chunk(c + 2)

        def l2_grpB(t_i):
            par = t_i % 2

            def fv2b(v, t_i=t_i, par=par):
                v.wait_ge(ss.h, rec[("sq3", t_i)])
                i = v.reciprocal(out=rsig2[:, par, :], in_=sqv2[:, par, :])
                vinc(sv, i)
                v.wait_ge(sv.h, sv.n)
                if t_i >= 2:
                    w = rec[("xtr", t_i - 2)]
                    v.wait_ge(w[0].h, w[1])
                i = v.tensor_scalar(out=xn2_bf[:, par, :], in0=x1_buf[:, par, :],
                                    scalar1=mvb2[:, par, 0:1],
                                    scalar2=rsig2[:, par, :],
                                    op0=ALU.subtract, op1=ALU.mult)
                vinc(sv, i)
            block.vector(fv2b)
            rec[("xn2", t_i)] = sv.n

            sem_tr = dtp[t_i % 2]

            def fs3b(s, t_i=t_i, par=par, sem=sem_tr, wv=rec[("xn2", t_i)],
                     w_self=sem_tr.n):
                s.wait_ge(sv.h, wv)
                s.wait_ge(sem.h, w_self)
                s.dma_start_transpose(
                    out=xn2T[:, :, t_i * P:(t_i + 1) * P],
                    in_=xn2_bf[:, par, :]).then_inc(sem.h, 16)
            s_add(fs3b)
            sem_tr.n += 16
            rec[("xtr", t_i)] = (sem_tr, sem_tr.n)
            if t_i % 4 == 3:
                rec[("xtr4", t_i // 4)] = (dtp[0].n, dtp[1].n)

        for t_i in range(NT_OWN + 1):
            if t_i < NT_OWN:
                l2_grpA(t_i)
            if t_i >= 1:
                l2_grpB(t_i - 1)
        s_flush()

        l2_done = dict(st=0, sv=rec[("xn2", NT_OWN - 1)],
                       ss=rec[("sq3", NT_OWN - 1)],
                       dt0=dtm[0].n, dt1=dtm[1].n,
                       da0=da[0].n, da1=da[1].n,
                       tr=[(s_, s_.n) for s_ in (dtp[0], dtp[1])])
        ctx2.close()

        wdT_sb = ctx34.enter_context(
            nc.sbuf_tensor("wdT_sb", [P, HC, D], BF16, side="right"))

        wgT_r = wgT_d.rearrange("(j p) h -> p j h", p=P)
        wuT_r = wuT_d.rearrange("(j p) h -> p j h", p=P)
        wdT_r = wdT_d.rearrange("(c p) d -> p c d", p=P)

        def ffn_wload(hb):
            sem = dwP[hb % 2]
            par = hb % 2
            w_war = rec.get(("up_mm", hb - 2, 3), 0)
            first = (hb <= 1)

            def f(gp, hb=hb, sem=sem, par=par, w_war=w_war, first=first):
                if first:
                    gp.wait_ge(st.h, pc_done["st"])
                    gp.wait_ge(sv.h, pc_done["sv"])
                    gp.wait_ge(ss.h, pc_done["ss"])
                    for s_, v_ in pc_done["tr"]:
                        gp.wait_ge(s_.h, v_)
                    gp.wait_ge(deL[0].h, deL[0].n)
                    gp.wait_ge(deL[1].h, deL[1].n)
                    gp.wait_ge(dsm[0].h, dsm[0].n)
                    gp.wait_ge(dsm[1].h, dsm[1].n)
                if w_war:
                    gp.wait_ge(st.h, w_war)
                gp.dma_start(out=wg_sb[:, par],
                             in_=wgT_r[:, :, hb * P:(hb + 1) * P]
                             ).then_inc(sem.h, 16)
                gp.dma_start(out=wu_sb[:, par],
                             in_=wuT_r[:, :, hb * P:(hb + 1) * P]
                             ).then_inc(sem.h, 16)
            block.gpsimd(f)
            sem.n += 32
            rec[("wload", hb)] = (sem, sem.n)

        def wdT_part(i4):
            def f(gp, i4=i4):
                if i4 == 0:
                    gp.wait_ge(sv.h, l2_done["sv"])
                    gp.wait_ge(ss.h, l2_done["ss"])
                    gp.wait_ge(dtm[0].h, l2_done["dt0"])
                    gp.wait_ge(dtm[1].h, l2_done["dt1"])
                    gp.wait_ge(da[0].h, l2_done["da0"])
                    gp.wait_ge(da[1].h, l2_done["da1"])
                    for s_, v_ in l2_done["tr"]:
                        gp.wait_ge(s_.h, v_)
                gp.dma_start(out=wdT_sb[:, i4 * 6:(i4 + 1) * 6, :],
                             in_=wdT_r[:, i4 * 6:(i4 + 1) * 6, :]
                             ).then_inc(dwF.h, 16)
            block.gpsimd(f)
            dwF.n += 16

        ffn_wload(0)
        ffn_wload(1)
        wdT_part(0)

        for hb in range(HC):
            par = hb % 2
            for tb in range(4):
                def ftg(t, hb=hb, par=par, tb=tb, w4=rec[("xtr4", tb)]):
                    if tb == 0:
                        w = rec[("wload", hb)]
                        t.wait_ge(w[0].h, w[1])
                    if hb == 0:
                        t.wait_ge(dtp[0].h, w4[0])
                        t.wait_ge(dtp[1].h, w4[1])
                    if hb >= 1:
                        t.wait_ge(ss.h, rec[("silu", hb - 1, tb)])
                    for j in range(KC):
                        i = t.matmul(ps_g[:, tb, :], wg_sb[:, par, j, :],
                                     xn2T[:, j, tb * 512:(tb + 1) * 512],
                                     start=(j == 0), stop=(j == KC - 1))
                    vinc(st, i)
                block.tensor(ftg)
                rec[("g_mm", hb, tb)] = st.n
            for tb in range(4):
                def ftu(t, hb=hb, par=par, tb=tb):
                    if hb >= 1:
                        t.wait_ge(sv.h, rec[("hmul", hb - 1, tb)])
                    for j in range(KC):
                        i = t.matmul(ps_u[:, tb, :], wu_sb[:, par, j, :],
                                     xn2T[:, j, tb * 512:(tb + 1) * 512],
                                     start=(j == 0), stop=(j == KC - 1))
                    vinc(st, i)
                block.tensor(ftu)
                rec[("up_mm", hb, tb)] = st.n
            for tb in range(4):
                prev_h = (rec.get(("hmul", hb, tb - 2), 0) if tb >= 2
                          else rec.get(("hmul", hb - 1, tb + 2), 0))

                def fsl(s, hb=hb, tb=tb, wt=rec[("g_mm", hb, tb)], prev=prev_h):
                    s.wait_ge(st.h, wt)
                    if prev:
                        s.wait_ge(sv.h, prev)
                    s.activation(out=silu_b[:, tb % 2, :], in_=ps_g[:, tb, :],
                                 func=AF.Silu).then_inc(ss.h, 1)
                s_add(fsl)
                ss.n += 1
                rec[("silu", hb, tb)] = ss.n

                def fvh(v, hb=hb, tb=tb):
                    v.wait_ge(ss.h, rec[("silu", hb, tb)])
                    v.wait_ge(st.h, rec[("up_mm", hb, tb)])
                    i = v.scalar_tensor_tensor(
                        out=h_act[:, hb, tb * 512:(tb + 1) * 512],
                        in0=silu_b[:, tb % 2, :], scalar=INV127,
                        in1=ps_u[:, tb, :], op0=ALU.mult, op1=ALU.mult)
                    vinc(sv, i)
                block.vector(fvh)
                rec[("hmul", hb, tb)] = sv.n
            if hb + 2 < HC:
                ffn_wload(hb + 2)
            if hb == 6:
                wdT_part(1)
            if hb == 12:
                wdT_part(2)
            if hb == 18:
                wdT_part(3)

        s_flush()
        hmul_done = rec[("hmul", HC - 1, 3)]
        ffn_done = dict(st=rec[("up_mm", HC - 1, 3)],
                        ss=rec[("silu", HC - 1, 3)], sv=hmul_done)
        ctx23.close()
        ctxPS.close()

        ctx4 = ExitStack()
        e4 = ctx4.enter_context
        xs3 = e4(nc.sbuf_tensor("xs3", [P, 2, D], F32))
        ln4_b = e4(nc.sbuf_tensor("ln4_b", [P, 2, D], F32))
        out_b = e4(nc.sbuf_tensor("out_b", [P, 2, D], F32))
        st64 = e4(nc.sbuf_tensor("st64", [P, 2, 12], F32))
        mvb4 = e4(nc.sbuf_tensor("mvb4", [P, 2, 2], F32))
        sqv4 = e4(nc.sbuf_tensor("sqv4", [P, 2, 1], F32))
        rsig4 = e4(nc.sbuf_tensor("rsig4", [P, 2, 1], F32))
        ps_d = e4(nc.psum_tensor("ps_d", [P, 2, D], F32))

        x1_r = x1_d.rearrange("(q p) d -> q p d", p=P)

        def l4_x1(t_i):
            sem = da[t_i % 2]
            w1 = rec[("x1_dma", t_i)]
            w2 = rec.get(("outadd", t_i - 2), 0)
            first = (t_i <= 1)

            def f(sync, t_i=t_i, sem=sem, w1=w1, w2=w2, first=first):
                sync.wait_ge(w1[0].h, w1[1])
                if first:
                    sync.wait_ge(st.h, ffn_done["st"])
                    sync.wait_ge(ss.h, ffn_done["ss"])
                    sync.wait_ge(sv.h, ffn_done["sv"])
                if w2:
                    sync.wait_ge(sv.h, w2)
                sync.dma_start(out=xs3[:, t_i % 2, :], in_=x1_r[t_i]
                               ).then_inc(sem.h, 16)
            block.sync(f)
            sem.n += 16
            rec[("x1rd", t_i)] = (sem, sem.n)

        l4_x1(0)
        l4_x1(1)

        for t_i in range(NT_OWN):
            par = t_i % 2

            def ftd(t, t_i=t_i, par=par):
                if t_i == 0:
                    t.wait_ge(sv.h, hmul_done)
                    t.wait_ge(dwF.h, 64)
                if t_i >= 2:
                    t.wait_ge(sv.h, rec[("d_evac", t_i - 2)])
                for nb in range(2):
                    for cb in range(HC):
                        i = t.matmul(ps_d[:, par, nb * 512:(nb + 1) * 512],
                                     h_act[:, cb, t_i * P:(t_i + 1) * P],
                                     wdT_sb[:, cb, nb * 512:(nb + 1) * 512],
                                     start=(cb == 0), stop=(cb == HC - 1))
                vinc(st, i)
            block.tensor(ftd)
            rec[("d_mm", t_i)] = st.n

            def fvd(v, t_i=t_i, par=par):
                v.wait_ge(st.h, rec[("d_mm", t_i)])
                v.bn_stats(out=st64[:, par, 0:6], in_=ps_d[:, par, 0:512])
                i = v.bn_stats(out=st64[:, par, 6:12], in_=ps_d[:, par, 512:1024])
                vinc(sv, i)
                v.wait_ge(sv.h, sv.n)
                i = v.bn_aggr(out=mvb4[:, par, :], in_=st64[:, par, :])
                vinc(sv, i)
            block.vector(fvd)
            rec[("d_bn", t_i)] = sv.n

            def fsd(s, t_i=t_i, par=par, wv=rec[("d_bn", t_i)]):
                s.wait_ge(sv.h, wv)
                s.activation(out=sqv4[:, par, :], in_=mvb4[:, par, 1:2],
                             func=AF.Sqrt, bias=eps_col[:]).then_inc(ss.h, 1)
            s_add(fsd)
            ss.n += 1
            rec[("sq4", t_i)] = ss.n

            def fvd2(v, t_i=t_i, par=par):
                v.wait_ge(ss.h, rec[("sq4", t_i)])
                i = v.reciprocal(out=rsig4[:, par, :], in_=sqv4[:, par, :])
                vinc(sv, i)
                v.wait_ge(sv.h, sv.n)
                if t_i >= 2:
                    w = rec[("outdma", t_i - 2)]
                    v.wait_ge(w[0].h, w[1])
                i = v.tensor_scalar(out=ln4_b[:, par, :], in0=ps_d[:, par, :],
                                    scalar1=mvb4[:, par, 0:1],
                                    scalar2=rsig4[:, par, :],
                                    op0=ALU.subtract, op1=ALU.mult)
                vinc(sv, i)
                rec[("d_evac", t_i)] = sv.n
                v.wait_ge(sv.h, sv.n)
                w = rec[("x1rd", t_i)]
                v.wait_ge(w[0].h, w[1])
                i = v.tensor_tensor(out=out_b[:, par, :], in0=ln4_b[:, par, :],
                                    in1=xs3[:, par, :], op=ALU.add)
                vinc(sv, i)
            block.vector(fvd2)
            rec[("outadd", t_i)] = sv.n
            if t_i + 2 < NT_OWN:
                l4_x1(t_i + 2)

            def fod(sync, t_i=t_i, par=par):
                sem = do_[t_i % 2]
                sync.wait_ge(sv.h, rec[("outadd", t_i)])
                sync.dma_start(out=out_d.rearrange("(q p) d -> q p d", p=P)[t_i],
                               in_=out_b[:, par, :]).then_inc(sem.h, 16)
            block.sync(fod)
            do_[t_i % 2].n += 16
            rec[("outdma", t_i)] = (do_[t_i % 2], do_[t_i % 2].n)

        s_flush()
        ctx4.close()
        ctx34.close()

        all_dma = [da[0], da[1], db, do_[0], do_[1], dsm[0], dsm[1],
                   dtm[0], dtm[1], dwA, dwB, dwC, dwP[0], dwP[1], dwF,
                   dtp[0], dtp[1], dts[0], dts[1], deG[0], deG[1],
                   deL[0], deL[1]]

        def fin(sync):
            for s_ in all_dma:
                if s_.n:
                    sync.wait_ge(s_.h, s_.n)
        block.sync(fin)

    return nc


_NC_CACHE = None


def _get_nc():
    global _NC_CACHE
    if _NC_CACHE is None:
        _NC_CACHE = build_program()
    return _NC_CACHE


def kernel(L0_attn_x, L0_attn_W_spin, L0_attn_b_spin, L0_attn_W_global,
           L0_attn_W_J, L0_attn_beta, L0_ffn_w_gate, L0_ffn_w_up,
           L0_ffn_w_down):
    x = np.asarray(L0_attn_x, np.float32)
    wsp = np.asarray(L0_attn_W_spin, np.float32)
    bsp = np.asarray(L0_attn_b_spin, np.float32)
    wgl = np.asarray(L0_attn_W_global, np.float32)
    wj = np.asarray(L0_attn_W_J, np.float32)
    beta = np.asarray(L0_attn_beta, np.float32).reshape(-1)[0]
    wg = np.asarray(L0_ffn_w_gate, np.int32)
    wu = np.asarray(L0_ffn_w_up, np.int32)
    wd = np.asarray(L0_ffn_w_down, np.int32)

    wspT = np.ascontiguousarray(wsp.T)
    wjT = np.ascontiguousarray(wj.T)
    wglT = np.ascontiguousarray(wgl.T)
    wgT = np.ascontiguousarray(wg.T)
    wuT = np.ascontiguousarray(wu.T)
    wdT = np.ascontiguousarray(wd.T)
    b_bc = np.ascontiguousarray(np.broadcast_to(bsp[None, :], (P, D)))
    beta_col = np.full((P, 1), beta, np.float32)
    ident = np.eye(P, dtype=ml_dtypes.bfloat16)

    in_maps = []
    for c in range(8):
        b, h = c // 2, c % 2
        xp = np.ascontiguousarray(
            np.concatenate([x[b, h * T:(h + 1) * T],
                            x[b, (1 - h) * T:(2 - h) * T]], 0))
        in_maps.append({
            "x": xp, "wspT": wspT, "wjT": wjT, "wglT": wglT,
            "b_bc": b_bc, "beta_col": beta_col, "ident": ident,
            "wgT": wgT, "wuT": wuT, "wdT": wdT,
        })

    nc = _get_nc()
    res = run_bass_kernel_spmd(nc, in_maps, core_ids=list(range(8)))
    out = np.empty((4, N, D), np.float32)
    for c in range(8):
        b, h = c // 2, c % 2
        out[b, h * T:(h + 1) * T] = res.results[c]["out"]
    return out


# version 17
# speedup vs baseline: 1.1254x; 1.0249x over previous
import numpy as np
import ml_dtypes
from contextlib import ExitStack

import concourse.bass as bass
import concourse.mybir as mybir
from concourse.bass_utils import run_bass_kernel_spmd

F32 = mybir.dt.float32
BF16 = mybir.dt.bfloat16
I32 = mybir.dt.int32
AF = mybir.ActivationFunctionType
ALU = mybir.AluOpType
AX = mybir.AxisListType

P = 128
N = 4096
T = 2048
D = 1024
H = 3072
KC = D // P
HC = H // P
NT_ALL = N // P
NT_OWN = T // P
EPS = 1e-5
INV32 = 1.0 / 32.0
INV127 = 1.0 / 127.0


class Sem:
    def __init__(self, ctx, nc, name):
        self.h = ctx.enter_context(nc.semaphore(name))
        self.n = 0


def build_program():
    nc = bass.Bass("TRN2", target_bir_lowering=False)

    x_d = nc.dram_tensor("x", [N, D], F32, kind="ExternalInput").ap()
    wspT_d = nc.dram_tensor("wspT", [D, D], F32, kind="ExternalInput").ap()
    wjT_d = nc.dram_tensor("wjT", [D, D], F32, kind="ExternalInput").ap()
    wglT_d = nc.dram_tensor("wglT", [D, D], F32, kind="ExternalInput").ap()
    bbc_d = nc.dram_tensor("b_bc", [P, D], F32, kind="ExternalInput").ap()
    beta_d = nc.dram_tensor("beta_col", [P, 1], F32, kind="ExternalInput").ap()
    ident_d = nc.dram_tensor("ident", [P, P], BF16, kind="ExternalInput").ap()
    wgT_d = nc.dram_tensor("wgT", [D, H], I32, kind="ExternalInput").ap()
    wuT_d = nc.dram_tensor("wuT", [D, H], I32, kind="ExternalInput").ap()
    wdT_d = nc.dram_tensor("wdT", [H, D], I32, kind="ExternalInput").ap()
    out_d = nc.dram_tensor("out", [T, D], F32, kind="ExternalOutput").ap()
    attn_d = nc.dram_tensor("attn_sc", [T, D], BF16, kind="ExternalOutput").ap()
    x1_d = nc.dram_tensor("x1_sc", [T, D], F32, kind="ExternalOutput").ap()
    e_d = nc.dram_tensor("e_sc", [NT_OWN, P, KC, P], BF16,
                         kind="ExternalOutput").ap()

    ctx = ExitStack()
    with ctx:
        e = ctx.enter_context

        da = [Sem(ctx, nc, "da0"), Sem(ctx, nc, "da1")]
        db = Sem(ctx, nc, "db")
        do_ = [Sem(ctx, nc, "do0"), Sem(ctx, nc, "do1")]
        dsm = [Sem(ctx, nc, "ds0"), Sem(ctx, nc, "ds1")]
        dtm = [Sem(ctx, nc, "dt0"), Sem(ctx, nc, "dt1")]
        dwA = Sem(ctx, nc, "dwA")
        dwB = Sem(ctx, nc, "dwB")
        dwC = Sem(ctx, nc, "dwC")
        dwP = [Sem(ctx, nc, "dwD"), Sem(ctx, nc, "dwE")]
        dwF = Sem(ctx, nc, "dwF")
        dtp = [Sem(ctx, nc, "dtp0"), Sem(ctx, nc, "dtp1")]
        dts = [Sem(ctx, nc, "dts0"), Sem(ctx, nc, "dts1")]
        deG = [Sem(ctx, nc, "deG0"), Sem(ctx, nc, "deG1")]
        deL = [Sem(ctx, nc, "deL0"), Sem(ctx, nc, "deL1")]
        sv = Sem(ctx, nc, "sv")
        ss = Sem(ctx, nc, "ss")
        st = Sem(ctx, nc, "st")

        def vinc(sem, instr, k=1):
            instr.then_inc(sem.h, k)
            sem.n += k
            return sem.n

        ident = e(nc.sbuf_tensor("ident_sb", [P, P], BF16))
        beta_col = e(nc.sbuf_tensor("beta_sb", [P, 1], F32))
        ones_col = e(nc.sbuf_tensor("ones_col", [P, 1], BF16))
        ones_row = e(nc.sbuf_tensor("ones_row", [1, P], BF16))
        g_bc = e(nc.sbuf_tensor("g_bc", [P, D], F32))
        eps_col = e(nc.sbuf_tensor("eps_col", [P, 1], F32))
        c_inv32 = e(nc.sbuf_tensor("c_inv32", [P, 1], F32))
        c_sc127 = e(nc.sbuf_tensor("c_sc127", [P, 1], F32))
        c_b127 = e(nc.sbuf_tensor("c_b127", [P, 1], F32))
        mv2_all = e(nc.sbuf_tensor("mv2_all", [P, NT_OWN, 2], F32))

        block = e(nc.Block())
        rec = {}
        sdefer = []

        def s_add(fn):
            sdefer.append(fn)

        def s_flush():
            fns = list(sdefer)
            del sdefer[:]
            if fns:
                def run(s, fns=fns):
                    for f in fns:
                        f(s)
                block.scalar(run)

        def boot_vec(vector):
            vector.memset(ones_col[:], 1.0)
            vector.memset(ones_row[:], 1.0)
            vector.memset(eps_col[:], EPS)
            vector.memset(c_inv32[:], INV32)
            vector.memset(c_sc127[:], 127.0 * 127.0)
            m = vector.memset(c_b127[:], 127.0 * 127.0 * EPS)
            vinc(sv, m)

        ctxA = ExitStack()
        eA = ctxA.enter_context
        s_tok = eA(nc.sbuf_tensor("s_tok", [P, NT_ALL, D], BF16))
        s_FM = eA(nc.sbuf_tensor("s_FM", [P, KC, N], BF16))

        ctxA1 = ExitStack()
        e1 = ctxA1.enter_context
        wspT_sb = e1(nc.sbuf_tensor("wspT_sb", [P, KC, D], BF16))
        wjT_sb = e1(nc.sbuf_tensor("wjT_sb", [P, KC, D], BF16))
        e_stage = e1(nc.sbuf_tensor("e_stage", [P, 2, 512], BF16))
        b_row = e1(nc.sbuf_tensor("b_row", [1, D], BF16))
        xs = e1(nc.sbuf_tensor("xs", [P, 2, 2, D], F32))
        xn_bf = e1(nc.sbuf_tensor("xn_bf", [P, 2, D], BF16))
        xnT_sb = e1(nc.sbuf_tensor("xnT_sb", [P, 2, D], BF16))
        st6 = e1(nc.sbuf_tensor("st6", [P, 2, 12], F32))
        mvb = e1(nc.sbuf_tensor("mvb", [P, 2, 2], F32))
        sqv = e1(nc.sbuf_tensor("sqv", [P, 2, 1], F32))
        rsig = e1(nc.sbuf_tensor("rsig", [P, 2, 1], F32))
        ps_s = e1(nc.psum_tensor("ps_s", [P, 2, D], F32))
        ps_e = e1(nc.psum_tensor("ps_e", [P, 2, 512], F32))

        def boot_sync(sync):
            sync.dma_start(out=ident[:], in_=ident_d).then_inc(db.h, 16)
            sync.dma_start(out=beta_col[:], in_=beta_d).then_inc(db.h, 16)
        block.sync(boot_sync)
        db.n += 32
        block.vector(boot_vec)
        sv_ones = sv.n

        def boot_gp(gp):
            gp.dma_start(out=wspT_sb[:],
                         in_=wspT_d.rearrange("(j p) d -> p j d", p=P)
                         ).then_inc(dwA.h, 16)
            gp.dma_start(out=b_row[:], in_=bbc_d[0:1, :]).then_inc(dwA.h, 16)
            gp.dma_start(out=wjT_sb[:],
                         in_=wjT_d.rearrange("(j p) d -> p j d", p=P)
                         ).then_inc(dwB.h, 16)
        block.gpsimd(boot_gp)
        dwA.n += 32
        dwB.n += 16

        x_r = x_d.rearrange("(c t p) d -> c p t d", t=2, p=P)

        def emit_xchunk(c):
            sem = da[c % 2]
            w_ss = rec.get(("apply", 2 * c - 3), 0)

            def f(sync, c=c, sem=sem, w_ss=w_ss):
                if w_ss:
                    sync.wait_ge(sv.h, w_ss)
                sync.dma_start(out=xs[:, c % 2], in_=x_r[c]).then_inc(sem.h, 16)
            block.sync(f)
            sem.n += 16
            rec[("xchunk", c)] = (sem, sem.n)

        emit_xchunk(0)
        emit_xchunk(1)

        def pa_stats(k):
            c, par = k // 2, k % 2
            xsl = xs[:, c % 2, par, :]
            w_sq = rec.get(("sqrt", k - 2), 0)

            def fvA(v, k=k, c=c, par=par, xsl=xsl, w_sq=w_sq):
                w = rec[("xchunk", c)]
                v.wait_ge(w[0].h, w[1])
                if w_sq:
                    v.wait_ge(ss.h, w_sq)
                v.bn_stats(out=st6[:, par, 0:6], in_=xsl[:, 0:512])
                i = v.bn_stats(out=st6[:, par, 6:12], in_=xsl[:, 512:1024])
                vinc(sv, i)
                v.wait_ge(sv.h, sv.n)
                i = v.bn_aggr(out=mvb[:, par, :], in_=st6[:, par, :])
                vinc(sv, i)
            block.vector(fvA)
            rec[("statsA", k)] = sv.n

            def fsq(s, k=k, par=par, wA=sv.n):
                s.wait_ge(sv.h, wA)
                s.activation(out=sqv[:, par, :], in_=mvb[:, par, 1:2],
                             func=AF.Sqrt, bias=eps_col[:]).then_inc(ss.h, 1)
            s_add(fsq)
            ss.n += 1
            rec[("sqrt", k)] = ss.n

        def pa_apply(k):
            c, par = k // 2, k % 2
            xsl = xs[:, c % 2, par, :]
            w_tr = rec.get(("xntr", k - 2), 0)

            def fvB(v, k=k, par=par, xsl=xsl, w_tr=w_tr):
                v.wait_ge(ss.h, rec[("sqrt", k)])
                i = v.reciprocal(out=rsig[:, par, :], in_=sqv[:, par, :])
                vinc(sv, i)
                v.wait_ge(sv.h, sv.n)
                if w_tr:
                    v.wait_ge(w_tr[0].h, w_tr[1])
                i = v.tensor_scalar(out=xn_bf[:, par, :], in0=xsl,
                                    scalar1=mvb[:, par, 0:1],
                                    scalar2=rsig[:, par, :],
                                    op0=ALU.subtract, op1=ALU.mult)
                vinc(sv, i)
            block.vector(fvB)
            rec[("apply", k)] = sv.n

        def pa_xntr(k):
            par = k % 2
            sem = dtp[par]
            w_mm = rec.get(("mm", k - 2), 0)

            def fdt(s, k=k, par=par, sem=sem, wv=rec[("apply", k)], w_mm=w_mm):
                s.wait_ge(sv.h, wv)
                if w_mm:
                    s.wait_ge(st.h, w_mm)
                s.dma_start_transpose(
                    out=xnT_sb[:, par, :].rearrange("p (j q) -> p j q", j=KC),
                    in_=xn_bf[:, par, :]).then_inc(sem.h, 16)
            s_add(fdt)
            sem.n += 16
            rec[("xntr", k)] = (sem, sem.n)

        def pa_mm(k):
            par = k % 2
            w_bias = rec.get(("biasS", k - 2), 0)

            def ft2(t, k=k, par=par, w_bias=w_bias):
                w = rec[("xntr", k)]
                t.wait_ge(w[0].h, w[1])
                if k == 0:
                    t.wait_ge(dwA.h, 32)
                    t.wait_ge(sv.h, sv_ones)
                if w_bias:
                    t.wait_ge(ss.h, w_bias)
                for nb in range(2):
                    t.matmul(ps_s[:, par, nb * 512:(nb + 1) * 512],
                             ones_row[:], b_row[0:1, nb * 512:(nb + 1) * 512],
                             start=True, stop=False)
                    for j in range(KC):
                        i = t.matmul(ps_s[:, par, nb * 512:(nb + 1) * 512],
                                     xnT_sb[:, par, j * P:(j + 1) * P],
                                     wspT_sb[:, j, nb * 512:(nb + 1) * 512],
                                     start=False, stop=(j == KC - 1))
                vinc(st, i)
            block.tensor(ft2)
            rec[("mm", k)] = st.n

        def pa_bias(k):
            def fsb(s, k=k, par=k % 2, wt=rec[("mm", k)]):
                s.wait_ge(st.h, wt)
                s.copy(out=s_tok[:, k, :], in_=ps_s[:, par, :]).then_inc(ss.h, 1)
            s_add(fsb)
            ss.n += 1
            rec[("biasS", k)] = ss.n

        def pa_stra(k):
            sem = dts[k % 2]

            def fdt(s, k=k, sem=sem, w_self=sem.n):
                s.wait_ge(ss.h, rec[("biasS", k)])
                s.wait_ge(sem.h, w_self)
                s.dma_start_transpose(
                    out=s_FM[:, :, k * P:(k + 1) * P],
                    in_=s_tok[:, k, :]).then_inc(sem.h, 16)
            s_add(fdt)
            sem.n += 16
            rec[("stra", k)] = (sem, sem.n)
            if k % 4 == 3:
                rec[("stra4", k // 4)] = (dts[0].n, dts[1].n)

        e_sc_w = e_d.rearrange("n p j t -> p n j t")
        eq_groups = [(m, tb) for tb in range(4) for m in range(KC)]
        eg_state = {"i": 0}
        pending_evac = []

        def emit_energy_mm(gi):
            m, tb = eq_groups[gi]
            slot = gi % 2

            def fte(t, m=m, tb=tb, slot=slot, gi=gi):
                if gi == 0:
                    t.wait_ge(dwB.h, 16)
                w4 = rec[("stra4", tb)]
                t.wait_ge(dts[0].h, w4[0])
                t.wait_ge(dts[1].h, w4[1])
                if gi >= 2:
                    t.wait_ge(sv.h, rec[("e_evac", gi - 2)])
                for j in range(KC):
                    i = t.matmul(ps_e[:, slot, :],
                                 wjT_sb[:, j, m * P:(m + 1) * P],
                                 s_FM[:, j, tb * 512:(tb + 1) * 512],
                                 start=(j == 0), stop=(j == KC - 1))
                vinc(st, i)
            block.tensor(fte)
            rec[("e_mm", gi)] = st.n

        def emit_energy_evac(gi):
            m, tb = eq_groups[gi]
            slot = gi % 2
            w_wr = rec.get(("e_wr", gi - 2), 0)

            def fve(v, m=m, tb=tb, slot=slot, gi=gi, w_wr=w_wr):
                v.wait_ge(st.h, rec[("e_mm", gi)])
                if w_wr:
                    v.wait_ge(w_wr[0].h, w_wr[1])
                i = v.tensor_copy(out=e_stage[:, slot, :], in_=ps_e[:, slot, :])
                vinc(sv, i)
            block.vector(fve)
            rec[("e_evac", gi)] = sv.n

            sem = deG[gi % 2]

            def fgw(gp, m=m, tb=tb, slot=slot, gi=gi, sem=sem):
                gp.wait_ge(sv.h, rec[("e_evac", gi)])
                gp.dma_start(
                    out=e_sc_w[:, 4 * tb:4 * tb + 4, m, :],
                    in_=e_stage[:, slot, :].rearrange("p (a t) -> p a t", a=4)
                    ).then_inc(sem.h, 16)
            block.gpsimd(fgw)
            sem.n += 16
            rec[("e_wr", gi)] = (sem, sem.n)
            if m == KC - 1:
                rec[("e_tb", tb)] = (deG[0].n, deG[1].n)

        for k in range(NT_ALL + 2):
            if k < NT_ALL:
                pa_stats(k)
            if 1 <= k < NT_ALL + 1:
                pa_apply(k - 1)
                pa_xntr(k - 1)
            if 2 <= k:
                pa_bias(k - 2)
                pa_stra(k - 2)
            for gi in pending_evac:
                emit_energy_evac(gi)
            del pending_evac[:]
            if 1 <= k < NT_ALL + 1:
                pa_mm(k - 1)
                if (k - 1) % 2 == 1:
                    c2 = (k - 1) // 2 + 2
                    if c2 < NT_ALL // 2:
                        emit_xchunk(c2)
            while (eg_state["i"] < len(eq_groups)
                   and ("stra4", eq_groups[eg_state["i"]][1]) in rec
                   and len(pending_evac) < 2
                   and eg_state["i"] < max(0, (k - 4)) * 2):
                emit_energy_mm(eg_state["i"])
                pending_evac.append(eg_state["i"])
                eg_state["i"] += 1
        for gi in pending_evac:
            emit_energy_evac(gi)
        del pending_evac[:]
        while eg_state["i"] < len(eq_groups):
            emit_energy_mm(eg_state["i"])
            emit_energy_evac(eg_state["i"])
            eg_state["i"] += 1
        s_flush()
        ctxA1.close()
        sv_pa_done = rec[("apply", NT_ALL - 1)]
        ss_pa_done = rec[("biasS", NT_ALL - 1)]
        pa_tr_done = [(s_, s_.n) for s_ in (dtp[0], dtp[1], dts[0], dts[1])]
        pa_done_st = rec[("mm", NT_ALL - 1)]

        sv_e_done = rec[("e_evac", KC * 4 - 1)]

        ctxC = ExitStack()
        eC = ctxC.enter_context
        P_sb = eC(nc.sbuf_tensor("P_sb", [P, N], BF16))
        PT_sb = eC(nc.sbuf_tensor("PT_sb", [P, 2, N], BF16))
        rsq = eC(nc.sbuf_tensor("rsq", [P, 2, 4], F32))
        rstot = eC(nc.sbuf_tensor("rstot", [P, 2, 1], F32))
        rinv = eC(nc.sbuf_tensor("rinv", [P, NT_OWN], F32))
        tmp1 = eC(nc.sbuf_tensor("tmp1", [P, D], F32))
        tmp2 = eC(nc.sbuf_tensor("tmp2", [P, D], F32))
        attn_wb = eC(nc.sbuf_tensor("attn_wb", [P, 2, D], BF16))
        st6c = eC(nc.sbuf_tensor("st6c", [P, 12], F32))
        ebuf = eC(nc.sbuf_tensor("ebuf", [P, 2, KC, P], BF16))
        wglT_sb = eC(nc.sbuf_tensor("wglT_sb", [P, KC, D], BF16))
        mf_bf = eC(nc.sbuf_tensor("mf_bf", [1, D], BF16))
        mfT_bf = eC(nc.sbuf_tensor("mfT_bf", [P, KC], BF16))
        g_bf = eC(nc.sbuf_tensor("g_bf", [1, D], BF16))
        ps_q = eC(nc.psum_tensor("ps_q", [P, 3, 1024], F32))
        ps_pv2 = eC(nc.psum_tensor("ps_pv2", [P, 1024], F32))
        ptx = [dtp[0], dtp[1], dts[0], dts[1]]

        def pcb_gp(gp):
            gp.wait_ge(sv.h, max(sv_pa_done, sv_e_done))
            gp.wait_ge(st.h, pa_done_st)
            gp.wait_ge(ss.h, ss_pa_done)
            for s_, v_ in pa_tr_done:
                gp.wait_ge(s_.h, v_)
            gp.dma_start(out=wglT_sb[:],
                         in_=wglT_d.rearrange("(j p) d -> p j d", p=P)
                         ).then_inc(dwC.h, 16)
        block.gpsimd(pcb_gp)
        dwC.n += 16

        gstate = {"sv_gbc": None}

        def emit_b_piece(idx):
            if idx in (0, 1):
                nb = idx

                def fmean(t, nb=nb):
                    if nb == 0:
                        t.wait_ge(ss.h, ss_pa_done)
                        t.wait_ge(sv.h, max(sv_pa_done, sv_e_done, sv_ones))
                        for s_, v_ in pa_tr_done:
                            t.wait_ge(s_.h, v_)
                    for k in range(NT_ALL):
                        i = t.matmul(ps_pv2[0:1, nb * 512:(nb + 1) * 512],
                                     ones_col[:],
                                     s_tok[:, k, nb * 512:(nb + 1) * 512],
                                     start=(k == 0), stop=(k == NT_ALL - 1))
                    vinc(st, i)
                block.tensor(fmean)
                rec[("b_mean", nb)] = st.n
            elif idx == 2:
                def fv1(v):
                    v.wait_ge(st.h, rec[("b_mean", 1)])
                    i = v.tensor_scalar_mul(out=mf_bf[:], in0=ps_pv2[0:1, :],
                                            scalar1=1.0 / N)
                    vinc(sv, i)
                block.vector(fv1)
                rec[("b_mf",)] = sv.n

                def ft2(t):
                    t.wait_ge(sv.h, rec[("b_mf",)])
                    for j in range(KC):
                        i = t.matmul(ps_pv2[:, j:j + 1],
                                     mf_bf[0:1, j * P:(j + 1) * P],
                                     ones_col[0:1, 0:1], start=True, stop=True)
                    vinc(st, i)
                block.tensor(ft2)
                rec[("b_mfT",)] = st.n

                def fv2(v):
                    v.wait_ge(st.h, rec[("b_mfT",)])
                    i = v.tensor_copy(out=mfT_bf[:], in_=ps_pv2[:, 0:KC])
                    vinc(sv, i)
                block.vector(fv2)
                rec[("b_mfTe",)] = sv.n
            elif idx == 3:
                def ft3(t):
                    t.wait_ge(sv.h, rec[("b_mfTe",)])
                    t.wait_ge(dwC.h, dwC.n)
                    for nb in range(2):
                        for j in range(KC):
                            i = t.matmul(ps_pv2[0:1, nb * 512:(nb + 1) * 512],
                                         mfT_bf[:, j:j + 1],
                                         wglT_sb[:, j, nb * 512:(nb + 1) * 512],
                                         start=(j == 0), stop=(j == KC - 1))
                    vinc(st, i)
                block.tensor(ft3)
                rec[("b_g",)] = st.n

                def fv3(v):
                    v.wait_ge(st.h, rec[("b_g",)])
                    i = v.tensor_copy(out=g_bf[:], in_=ps_pv2[0:1, :])
                    vinc(sv, i)
                block.vector(fv3)
                rec[("b_gbf",)] = sv.n
            elif idx == 4:
                for nb in range(2):
                    def ftb(t, nb=nb):
                        t.wait_ge(sv.h, rec[("b_gbf",)] if nb == 0
                                  else rec[("b_gbc", 0)])
                        i = t.matmul(ps_pv2[:, 0:512], ones_row[:],
                                     g_bf[0:1, nb * 512:(nb + 1) * 512],
                                     start=True, stop=True)
                        vinc(st, i)
                    block.tensor(ftb)
                    rec[("b_gb", nb)] = st.n

                    def fvb(v, nb=nb):
                        v.wait_ge(st.h, rec[("b_gb", nb)])
                        i = v.tensor_copy(out=g_bc[:, nb * 512:(nb + 1) * 512],
                                          in_=ps_pv2[:, 0:512])
                        vinc(sv, i)
                    block.vector(fvb)
                    rec[("b_gbc", nb)] = sv.n
                gstate["sv_gbc"] = rec[("b_gbc", 1)]

        def e_load(qt):
            sem = deL[qt % 2]
            tbw = rec[("e_tb", qt // 4)]
            w_sc = rec.get(("scq", qt - 2, 3), 0)

            def f(sync, qt=qt, sem=sem, tbw=tbw, w_sc=w_sc):
                sync.wait_ge(deG[0].h, tbw[0])
                sync.wait_ge(deG[1].h, tbw[1])
                if qt <= 1:
                    sync.wait_ge(sv.h, max(sv_pa_done, sv_e_done))
                    sync.wait_ge(ss.h, ss_pa_done)
                    sync.wait_ge(st.h, pa_done_st)
                    for s_, v_ in pa_tr_done:
                        sync.wait_ge(s_.h, v_)
                    sync.wait_ge(deG[0].h, deG[0].n)
                    sync.wait_ge(deG[1].h, deG[1].n)
                if w_sc:
                    sync.wait_ge(st.h, w_sc)
                sync.dma_start(out=ebuf[:, qt % 2], in_=e_d[qt]
                               ).then_inc(sem.h, 16)
            block.sync(f)
            sem.n += 16
            rec[("e_ld", qt)] = (sem, sem.n)

        e_load(0)
        e_load(1)

        def c_scq(qt, qi):
            qg = 4 * qt + qi
            slot = qg % 3
            w_slot = rec.get(("expg", qg - 3), 0)

            def ftsq(t, qt=qt, qi=qi, slot=slot, w_slot=w_slot):
                if qi == 0:
                    w = rec[("e_ld", qt)]
                    t.wait_ge(w[0].h, w[1])
                    if qt == 0:
                        t.wait_ge(ss.h, ss_pa_done)
                        t.wait_ge(sv.h, max(sv_pa_done, sv_e_done))
                if w_slot:
                    t.wait_ge(ss.h, w_slot)
                for half in range(2):
                    for j in range(KC):
                        i = t.matmul(
                            ps_q[:, slot, half * 512:(half + 1) * 512],
                            ebuf[:, qt % 2, j, :],
                            s_FM[:, j, qi * 1024 + half * 512:
                                 qi * 1024 + (half + 1) * 512],
                            start=(j == 0), stop=(j == KC - 1))
                vinc(st, i)
            block.tensor(ftsq)
            rec[("scq", qt, qi)] = st.n

        def c_fexp(qt, qi):
            qg = 4 * qt + qi
            slot = qg % 3
            cp = qt % 2

            def fexp(s, qt=qt, qi=qi, slot=slot, cp=cp,
                     wt=rec[("scq", qt, qi)],
                     wtr=rec.get(("ptr", qt - 1, qi), 0),
                     wrs=rec.get(("rsfix", qt - 1), 0)):
                s.wait_ge(st.h, wt)
                if wtr:
                    s.wait_ge(wtr[0].h, wtr[1])
                if qi == 0 and wrs:
                    s.wait_ge(sv.h, wrs)
                s.activation(out=P_sb[:, qi * 1024:(qi + 1) * 1024],
                             in_=ps_q[:, slot, :], func=AF.Exp,
                             bias=0.0, scale=c_inv32[:],
                             accum_out=rsq[:, cp, qi:qi + 1]).then_inc(ss.h, 1)
            s_add(fexp)
            ss.n += 1
            rec[("expq", qt, qi)] = ss.n
            rec[("expg", 4 * qt + qi)] = ss.n

        def c_fptr(qt, qi):
            par = qt % 2
            sem = ptx[qi]
            w_pv = rec.get(("pvb", qt - 2, qi), 0)

            def fptr(sy, qt=qt, qi=qi, par=par, sem=sem, w_pv=w_pv,
                     wss=rec[("expq", qt, qi)], w_self=sem.n):
                sy.wait_ge(ss.h, wss)
                sy.wait_ge(sem.h, w_self)
                if w_pv:
                    sy.wait_ge(st.h, w_pv)
                sy.dma_start_transpose(
                    out=PT_sb[:, par, qi * 1024:(qi + 1) * 1024]
                    .rearrange("p (j q) -> p j q", j=8),
                    in_=P_sb[:, qi * 1024:(qi + 1) * 1024]
                    ).then_inc(sem.h, 16)
            block.sync(fptr)
            sem.n += 16
            rec[("ptr", qt, qi)] = (sem, sem.n)

        def c_quarter(qt, qi):
            c_scq(qt, qi)
            c_fexp(qt, qi)
            c_fptr(qt, qi)

        pend_bn2 = []
        c_quarter(0, 0)
        for qt in range(NT_OWN):
            if pend_bn2 and qt >= 1:
                for qt5, f5 in pend_bn2:
                    block.vector(f5)
                    rec[("bn2", qt5)] = sv.n
                del pend_bn2[:]
            par = qt % 2
            cp = qt % 2
            for qi in (1, 2, 3):
                c_quarter(qt, qi)
                if qt == 0:
                    emit_b_piece(qi - 1)
            if qt == 0:
                emit_b_piece(3)

            if qt + 2 < NT_OWN:
                e_load(qt + 2)

            if qt + 1 < NT_OWN:
                c_scq(qt + 1, 0)

            def fvr(v, qt=qt, cp=cp):
                v.wait_ge(ss.h, rec[("expq", qt, 3)])
                i = v.reduce_sum(out=rstot[:, cp, :], in_=rsq[:, cp, :],
                                 axis=AX.X)
                vinc(sv, i)
                v.wait_ge(sv.h, sv.n)
                i = v.reciprocal(out=rinv[:, qt:qt + 1], in_=rstot[:, cp, :])
                vinc(sv, i)
            block.vector(fvr)
            rec[("rsfix", qt)] = sv.n

            if qt == 0:
                emit_b_piece(4)
            sv_gbc = gstate["sv_gbc"]

            for bq in range(4):
                def ftv(t, qt=qt, bq=bq, par=par, sv_gbc=sv_gbc):
                    w = rec[("ptr", qt, bq)]
                    t.wait_ge(w[0].h, w[1])
                    if bq == 0 and qt >= 1:
                        t.wait_ge(sv.h, rec[("stt", qt - 1)])
                    if qt == 0 and bq == 0:
                        t.wait_ge(sv.h, sv_gbc)
                    for nb in range(2):
                        for cq in range(8):
                            cb = bq * 8 + cq
                            i = t.matmul(ps_pv2[:, nb * 512:(nb + 1) * 512],
                                         PT_sb[:, par, cb * P:(cb + 1) * P],
                                         s_tok[:, cb, nb * 512:(nb + 1) * 512],
                                         start=(cb == 0), stop=(cb == NT_ALL - 1))
                    vinc(st, i)
                block.tensor(ftv)
                rec[("pvb", qt, bq)] = st.n
            rec[("pv", qt)] = rec[("pvb", qt, 3)]

            if qt + 1 < NT_OWN:
                c_fexp(qt + 1, 0)
                c_fptr(qt + 1, 0)

            def fv4(v, qt=qt, par=par):
                v.wait_ge(st.h, rec[("pv", qt)])
                v.wait_ge(sv.h, rec[("rsfix", qt)])
                if qt == 0:
                    v.wait_ge(sv.h, sv_gbc)
                i = v.scalar_tensor_tensor(
                    out=tmp1[:], in0=ps_pv2[:], scalar=rinv[:, qt:qt + 1],
                    in1=g_bc[:], op0=ALU.mult, op1=ALU.add)
                vinc(sv, i)
                v.wait_ge(sv.h, sv.n)
                i = v.tensor_tensor(out=tmp2[:], in0=tmp1[:],
                                    in1=s_tok[:, qt, :], op=ALU.add)
                vinc(sv, i)
            block.vector(fv4)
            rec[("stt", qt)] = sv.n

            def fst(s, qt=qt, par=par, wv=rec[("stt", qt)],
                    wd=rec.get(("attn_dma", qt - 2), 0)):
                s.wait_ge(sv.h, wv)
                if qt == 0:
                    s.wait_ge(db.h, 32)
                if wd:
                    s.wait_ge(wd[0].h, wd[1])
                s.activation(out=attn_wb[:, par, :], in_=tmp2[:],
                             func=AF.Tanh, scale=beta_col[:]
                             ).then_inc(ss.h, 1)
            s_add(fst)
            ss.n += 1
            rec[("tanh", qt)] = ss.n

            def fv5(v, qt=qt, par=par):
                v.wait_ge(ss.h, rec[("tanh", qt)])
                v.bn_stats(out=st6c[:, 0:6], in_=attn_wb[:, par, 0:512])
                i = v.bn_stats(out=st6c[:, 6:12],
                               in_=attn_wb[:, par, 512:1024])
                vinc(sv, i)
                v.wait_ge(sv.h, sv.n)
                i = v.bn_aggr(out=mv2_all[:, qt, :], in_=st6c[:])
                vinc(sv, i)
            pend_bn2.append((qt, fv5))
            if qt == NT_OWN - 1:
                for qt5, f5 in pend_bn2:
                    block.vector(f5)
                    rec[("bn2", qt5)] = sv.n
                del pend_bn2[:]

            def fdma(s, qt=qt, par=par, wss=rec[("tanh", qt)]):
                s.wait_ge(ss.h, wss)
                s.dma_start(out=attn_d.rearrange("(q p) d -> q p d", p=P)[qt],
                            in_=attn_wb[:, par, :]).then_inc(dsm[qt % 2].h, 16)
            s_add(fdma)
            dsm[qt % 2].n += 16
            rec[("attn_dma", qt)] = (dsm[qt % 2], dsm[qt % 2].n)

        s_flush()
        pc_done = dict(st=rec[("pv", NT_OWN - 1)], sv=rec[("bn2", NT_OWN - 1)],
                       ss=rec[("tanh", NT_OWN - 1)],
                       tr=[(s_, s_.n) for s_ in (dtp[0], dtp[1], dts[0], dts[1])])
        ctxC.close()
        ctxA.close()

        ctx34 = ExitStack()
        h_act = ctx34.enter_context(
            nc.sbuf_tensor("h_act", [P, HC, T], BF16, side="right"))
        wg_sb = ctx34.enter_context(
            nc.sbuf_tensor("wg_sb", [P, 2, KC, P], BF16, side="right"))
        wu_sb = ctx34.enter_context(
            nc.sbuf_tensor("wu_sb", [P, 2, KC, P], BF16, side="right"))
        silu_b = ctx34.enter_context(
            nc.sbuf_tensor("silu_b", [P, 2, 512], F32, side="right"))
        ctxPS = ExitStack()
        ps_g = ctxPS.enter_context(nc.psum_tensor("ps_g", [P, 4, 512], F32))
        ps_u = ctxPS.enter_context(nc.psum_tensor("ps_u", [P, 4, 512], F32))

        ctx2 = ExitStack()
        e2 = ctx2.enter_context
        ctx23 = ExitStack()
        xn2T = ctx23.enter_context(nc.sbuf_tensor("xn2T", [P, KC, T], BF16))

        xs2 = e2(nc.sbuf_tensor("xs2", [P, 2, 2, D], F32))
        ab_buf = e2(nc.sbuf_tensor("ab_buf", [P, 2, 2, D], BF16))
        ln_tmp = e2(nc.sbuf_tensor("ln_tmp", [P, 2, D], F32))
        x1_buf = e2(nc.sbuf_tensor("x1_buf", [P, 2, D], F32))
        xn2_bf = e2(nc.sbuf_tensor("xn2_bf", [P, 2, D], BF16))
        sq2_all = e2(nc.sbuf_tensor("sq2_all", [P, NT_OWN], F32))
        rs2_all = e2(nc.sbuf_tensor("rs2_all", [P, NT_OWN], F32))
        st62 = e2(nc.sbuf_tensor("st62", [P, 2, 12], F32))
        mvb2 = e2(nc.sbuf_tensor("mvb2", [P, 2, 2], F32))
        sqv2 = e2(nc.sbuf_tensor("sqv2", [P, 2, 1], F32))
        rsig2 = e2(nc.sbuf_tensor("rsig2", [P, 2, 1], F32))

        def l2_s0(s):
            s.wait_ge(sv.h, pc_done["sv"])
            i = s.activation(out=sq2_all[:], in_=mv2_all[:, :, 1], func=AF.Sqrt,
                             bias=eps_col[:])
            vinc(ss, i)
        block.scalar(l2_s0)
        ss_sq2 = ss.n

        def l2_v0(v):
            v.wait_ge(ss.h, ss_sq2)
            i = v.reciprocal(out=rs2_all[:], in_=sq2_all[:])
            vinc(sv, i)
        block.vector(l2_v0)
        sv_l2s = sv.n

        attn_r = attn_d.rearrange("(c t p) d -> c p t d", t=2, p=P)
        x_own_r = x_d.rearrange("(c t p) d -> c p t d", t=2, p=P)

        def l2_chunk(c):
            sem = da[c % 2]
            w0 = rec[("attn_dma", 2 * c)]
            w1 = rec[("attn_dma", 2 * c + 1)]
            w2 = rec.get(("l2_lastread", c - 2), 0)
            first = (c <= 1)

            def f(sync, c=c, sem=sem, w0=w0, w1=w1, w2=w2, first=first):
                sync.wait_ge(w0[0].h, w0[1])
                sync.wait_ge(w1[0].h, w1[1])
                if first:
                    sync.wait_ge(st.h, pc_done["st"])
                    sync.wait_ge(sv.h, pc_done["sv"])
                    sync.wait_ge(ss.h, pc_done["ss"])
                    sync.wait_ge(deL[0].h, deL[0].n)
                    sync.wait_ge(deL[1].h, deL[1].n)
                    for s_, v_ in pc_done["tr"]:
                        sync.wait_ge(s_.h, v_)
                if w2:
                    sync.wait_ge(sv.h, w2)
                sync.dma_start(out=ab_buf[:, c % 2], in_=attn_r[c]
                               ).then_inc(sem.h, 16)
                sync.dma_start(out=xs2[:, c % 2], in_=x_own_r[c]
                               ).then_inc(sem.h, 16)
            block.sync(f)
            sem.n += 32
            rec[("l2chunk", c)] = (sem, sem.n)

        l2_chunk(0)
        l2_chunk(1)

        def l2_grpA(t_i):
            c, par = t_i // 2, t_i % 2
            w_sq = rec.get(("sq3", t_i - 2), 0)

            def fv1(v, t_i=t_i, c=c, par=par, w_sq=w_sq):
                w = rec[("l2chunk", c)]
                v.wait_ge(w[0].h, w[1])
                v.wait_ge(sv.h, sv_l2s)
                if w_sq:
                    v.wait_ge(ss.h, w_sq)
                i = v.tensor_scalar(out=ln_tmp[:, par, :],
                                    in0=ab_buf[:, c % 2, t_i % 2, :],
                                    scalar1=mv2_all[:, t_i, 0:1],
                                    scalar2=rs2_all[:, t_i:t_i + 1],
                                    op0=ALU.subtract, op1=ALU.mult)
                vinc(sv, i)
                v.wait_ge(sv.h, sv.n)
                if t_i >= 2:
                    w = rec[("x1_dma", t_i - 2)]
                    v.wait_ge(w[0].h, w[1])
                i = v.tensor_tensor(out=x1_buf[:, par, :], in0=ln_tmp[:, par, :],
                                    in1=xs2[:, c % 2, t_i % 2, :], op=ALU.add)
                vinc(sv, i)
                v.wait_ge(sv.h, sv.n)
                v.bn_stats(out=st62[:, par, 0:6], in_=x1_buf[:, par, 0:512])
                i = v.bn_stats(out=st62[:, par, 6:12],
                               in_=x1_buf[:, par, 512:1024])
                vinc(sv, i)
                v.wait_ge(sv.h, sv.n)
                i = v.bn_aggr(out=mvb2[:, par, :], in_=st62[:, par, :])
                vinc(sv, i)
            block.vector(fv1)
            rec[("x1", t_i)] = sv.n

            def fdma2(s, t_i=t_i, par=par, wv=sv.n):
                s.wait_ge(sv.h, wv)
                s.dma_start(out=x1_d.rearrange("(q p) d -> q p d", p=P)[t_i],
                            in_=x1_buf[:, par, :]).then_inc(dtm[t_i % 2].h, 16)
            s_add(fdma2)
            dtm[t_i % 2].n += 16
            rec[("x1_dma", t_i)] = (dtm[t_i % 2], dtm[t_i % 2].n)

            def fs2b(s, t_i=t_i, par=par, wv=sv.n):
                s.wait_ge(sv.h, wv)
                s.activation(out=sqv2[:, par, :], in_=mvb2[:, par, 1:2],
                             func=AF.Sqrt, scale=c_sc127[:], bias=c_b127[:]
                             ).then_inc(ss.h, 1)
            s_add(fs2b)
            ss.n += 1
            rec[("sq3", t_i)] = ss.n
            if t_i % 2 == 1:
                rec[("l2_lastread", c)] = rec[("x1", t_i)]
                if c + 2 < NT_OWN // 2:
                    l2_chunk(c + 2)

        def l2_grpB(t_i):
            par = t_i % 2

            def fv2b(v, t_i=t_i, par=par):
                v.wait_ge(ss.h, rec[("sq3", t_i)])
                i = v.reciprocal(out=rsig2[:, par, :], in_=sqv2[:, par, :])
                vinc(sv, i)
                v.wait_ge(sv.h, sv.n)
                if t_i >= 2:
                    w = rec[("xtr", t_i - 2)]
                    v.wait_ge(w[0].h, w[1])
                i = v.tensor_scalar(out=xn2_bf[:, par, :], in0=x1_buf[:, par, :],
                                    scalar1=mvb2[:, par, 0:1],
                                    scalar2=rsig2[:, par, :],
                                    op0=ALU.subtract, op1=ALU.mult)
                vinc(sv, i)
            block.vector(fv2b)
            rec[("xn2", t_i)] = sv.n

            sem_tr = dtp[t_i % 2]

            def fs3b(s, t_i=t_i, par=par, sem=sem_tr, wv=rec[("xn2", t_i)],
                     w_self=sem_tr.n):
                s.wait_ge(sv.h, wv)
                s.wait_ge(sem.h, w_self)
                s.dma_start_transpose(
                    out=xn2T[:, :, t_i * P:(t_i + 1) * P],
                    in_=xn2_bf[:, par, :]).then_inc(sem.h, 16)
            s_add(fs3b)
            sem_tr.n += 16
            rec[("xtr", t_i)] = (sem_tr, sem_tr.n)
            if t_i % 4 == 3:
                rec[("xtr4", t_i // 4)] = (dtp[0].n, dtp[1].n)

        for t_i in range(NT_OWN + 1):
            if t_i < NT_OWN:
                l2_grpA(t_i)
            if t_i >= 1:
                l2_grpB(t_i - 1)
        s_flush()

        l2_done = dict(st=0, sv=rec[("xn2", NT_OWN - 1)],
                       ss=rec[("sq3", NT_OWN - 1)],
                       dt0=dtm[0].n, dt1=dtm[1].n,
                       da0=da[0].n, da1=da[1].n,
                       tr=[(s_, s_.n) for s_ in (dtp[0], dtp[1])])
        ctx2.close()

        wdT_sb = ctx34.enter_context(
            nc.sbuf_tensor("wdT_sb", [P, HC, D], BF16, side="right"))

        wgT_r = wgT_d.rearrange("(j p) h -> p j h", p=P)
        wuT_r = wuT_d.rearrange("(j p) h -> p j h", p=P)
        wdT_r = wdT_d.rearrange("(c p) d -> p c d", p=P)

        def ffn_wload(hb):
            sem = dwP[hb % 2]
            par = hb % 2
            w_war = rec.get(("up_mm", hb - 2, 3), 0)
            first = (hb <= 1)

            def f(gp, hb=hb, sem=sem, par=par, w_war=w_war, first=first):
                if first:
                    gp.wait_ge(st.h, pc_done["st"])
                    gp.wait_ge(sv.h, pc_done["sv"])
                    gp.wait_ge(ss.h, pc_done["ss"])
                    for s_, v_ in pc_done["tr"]:
                        gp.wait_ge(s_.h, v_)
                    gp.wait_ge(deL[0].h, deL[0].n)
                    gp.wait_ge(deL[1].h, deL[1].n)
                    gp.wait_ge(dsm[0].h, dsm[0].n)
                    gp.wait_ge(dsm[1].h, dsm[1].n)
                if w_war:
                    gp.wait_ge(st.h, w_war)
                gp.dma_start(out=wg_sb[:, par],
                             in_=wgT_r[:, :, hb * P:(hb + 1) * P]
                             ).then_inc(sem.h, 16)
                gp.dma_start(out=wu_sb[:, par],
                             in_=wuT_r[:, :, hb * P:(hb + 1) * P]
                             ).then_inc(sem.h, 16)
            block.gpsimd(f)
            sem.n += 32
            rec[("wload", hb)] = (sem, sem.n)

        def wdT_part(i4):
            def f(gp, i4=i4):
                if i4 == 0:
                    gp.wait_ge(sv.h, l2_done["sv"])
                    gp.wait_ge(ss.h, l2_done["ss"])
                    gp.wait_ge(dtm[0].h, l2_done["dt0"])
                    gp.wait_ge(dtm[1].h, l2_done["dt1"])
                    gp.wait_ge(da[0].h, l2_done["da0"])
                    gp.wait_ge(da[1].h, l2_done["da1"])
                    for s_, v_ in l2_done["tr"]:
                        gp.wait_ge(s_.h, v_)
                gp.dma_start(out=wdT_sb[:, i4 * 6:(i4 + 1) * 6, :],
                             in_=wdT_r[:, i4 * 6:(i4 + 1) * 6, :]
                             ).then_inc(dwF.h, 16)
            block.gpsimd(f)
            dwF.n += 16

        ffn_wload(0)
        ffn_wload(1)
        wdT_part(0)

        for hb in range(HC):
            par = hb % 2
            for tb in range(4):
                def ftg(t, hb=hb, par=par, tb=tb, w4=rec[("xtr4", tb)]):
                    if tb == 0:
                        w = rec[("wload", hb)]
                        t.wait_ge(w[0].h, w[1])
                    if hb == 0:
                        t.wait_ge(dtp[0].h, w4[0])
                        t.wait_ge(dtp[1].h, w4[1])
                    if hb >= 1:
                        t.wait_ge(ss.h, rec[("silu", hb - 1, tb)])
                    for j in range(KC):
                        i = t.matmul(ps_g[:, tb, :], wg_sb[:, par, j, :],
                                     xn2T[:, j, tb * 512:(tb + 1) * 512],
                                     start=(j == 0), stop=(j == KC - 1))
                    vinc(st, i)
                block.tensor(ftg)
                rec[("g_mm", hb, tb)] = st.n
            for tb in range(4):
                def ftu(t, hb=hb, par=par, tb=tb):
                    if hb >= 1:
                        t.wait_ge(sv.h, rec[("hmul", hb - 1, tb)])
                    for j in range(KC):
                        i = t.matmul(ps_u[:, tb, :], wu_sb[:, par, j, :],
                                     xn2T[:, j, tb * 512:(tb + 1) * 512],
                                     start=(j == 0), stop=(j == KC - 1))
                    vinc(st, i)
                block.tensor(ftu)
                rec[("up_mm", hb, tb)] = st.n
            for tb in range(4):
                prev_h = (rec.get(("hmul", hb, tb - 2), 0) if tb >= 2
                          else rec.get(("hmul", hb - 1, tb + 2), 0))

                def fsl(s, hb=hb, tb=tb, wt=rec[("g_mm", hb, tb)], prev=prev_h):
                    s.wait_ge(st.h, wt)
                    if prev:
                        s.wait_ge(sv.h, prev)
                    s.activation(out=silu_b[:, tb % 2, :], in_=ps_g[:, tb, :],
                                 func=AF.Silu).then_inc(ss.h, 1)
                s_add(fsl)
                ss.n += 1
                rec[("silu", hb, tb)] = ss.n

                def fvh(v, hb=hb, tb=tb):
                    v.wait_ge(ss.h, rec[("silu", hb, tb)])
                    v.wait_ge(st.h, rec[("up_mm", hb, tb)])
                    i = v.scalar_tensor_tensor(
                        out=h_act[:, hb, tb * 512:(tb + 1) * 512],
                        in0=silu_b[:, tb % 2, :], scalar=INV127,
                        in1=ps_u[:, tb, :], op0=ALU.mult, op1=ALU.mult)
                    vinc(sv, i)
                block.vector(fvh)
                rec[("hmul", hb, tb)] = sv.n
            if hb + 2 < HC:
                ffn_wload(hb + 2)
            if hb == 6:
                wdT_part(1)
            if hb == 12:
                wdT_part(2)
            if hb == 18:
                wdT_part(3)

        s_flush()
        hmul_done = rec[("hmul", HC - 1, 3)]
        ffn_done = dict(st=rec[("up_mm", HC - 1, 3)],
                        ss=rec[("silu", HC - 1, 3)], sv=hmul_done)
        ctx23.close()
        ctxPS.close()

        ctx4 = ExitStack()
        e4 = ctx4.enter_context
        xs3 = e4(nc.sbuf_tensor("xs3", [P, 2, D], F32))
        ln4_b = e4(nc.sbuf_tensor("ln4_b", [P, 2, D], F32))
        out_b = e4(nc.sbuf_tensor("out_b", [P, 2, D], F32))
        st64 = e4(nc.sbuf_tensor("st64", [P, 2, 12], F32))
        mvb4 = e4(nc.sbuf_tensor("mvb4", [P, 2, 2], F32))
        sqv4 = e4(nc.sbuf_tensor("sqv4", [P, 2, 1], F32))
        rsig4 = e4(nc.sbuf_tensor("rsig4", [P, 2, 1], F32))
        ps_d = e4(nc.psum_tensor("ps_d", [P, 2, D], F32))

        x1_r = x1_d.rearrange("(q p) d -> q p d", p=P)

        def l4_x1(t_i):
            sem = da[t_i % 2]
            w1 = rec[("x1_dma", t_i)]
            w2 = rec.get(("outadd", t_i - 2), 0)
            first = (t_i <= 1)

            def f(sync, t_i=t_i, sem=sem, w1=w1, w2=w2, first=first):
                sync.wait_ge(w1[0].h, w1[1])
                if first:
                    sync.wait_ge(st.h, ffn_done["st"])
                    sync.wait_ge(ss.h, ffn_done["ss"])
                    sync.wait_ge(sv.h, ffn_done["sv"])
                if w2:
                    sync.wait_ge(sv.h, w2)
                sync.dma_start(out=xs3[:, t_i % 2, :], in_=x1_r[t_i]
                               ).then_inc(sem.h, 16)
            block.sync(f)
            sem.n += 16
            rec[("x1rd", t_i)] = (sem, sem.n)

        l4_x1(0)
        l4_x1(1)

        for t_i in range(NT_OWN):
            par = t_i % 2

            def ftd(t, t_i=t_i, par=par):
                if t_i == 0:
                    t.wait_ge(sv.h, hmul_done)
                    t.wait_ge(dwF.h, 64)
                if t_i >= 2:
                    t.wait_ge(sv.h, rec[("d_evac", t_i - 2)])
                for nb in range(2):
                    for cb in range(HC):
                        i = t.matmul(ps_d[:, par, nb * 512:(nb + 1) * 512],
                                     h_act[:, cb, t_i * P:(t_i + 1) * P],
                                     wdT_sb[:, cb, nb * 512:(nb + 1) * 512],
                                     start=(cb == 0), stop=(cb == HC - 1))
                vinc(st, i)
            block.tensor(ftd)
            rec[("d_mm", t_i)] = st.n

            def fvd(v, t_i=t_i, par=par):
                v.wait_ge(st.h, rec[("d_mm", t_i)])
                v.bn_stats(out=st64[:, par, 0:6], in_=ps_d[:, par, 0:512])
                i = v.bn_stats(out=st64[:, par, 6:12], in_=ps_d[:, par, 512:1024])
                vinc(sv, i)
                v.wait_ge(sv.h, sv.n)
                i = v.bn_aggr(out=mvb4[:, par, :], in_=st64[:, par, :])
                vinc(sv, i)
            block.vector(fvd)
            rec[("d_bn", t_i)] = sv.n

            def fsd(s, t_i=t_i, par=par, wv=rec[("d_bn", t_i)]):
                s.wait_ge(sv.h, wv)
                s.activation(out=sqv4[:, par, :], in_=mvb4[:, par, 1:2],
                             func=AF.Sqrt, bias=eps_col[:]).then_inc(ss.h, 1)
            s_add(fsd)
            ss.n += 1
            rec[("sq4", t_i)] = ss.n

            def fvd2(v, t_i=t_i, par=par):
                v.wait_ge(ss.h, rec[("sq4", t_i)])
                i = v.reciprocal(out=rsig4[:, par, :], in_=sqv4[:, par, :])
                vinc(sv, i)
                v.wait_ge(sv.h, sv.n)
                if t_i >= 2:
                    w = rec[("outdma", t_i - 2)]
                    v.wait_ge(w[0].h, w[1])
                i = v.tensor_scalar(out=ln4_b[:, par, :], in0=ps_d[:, par, :],
                                    scalar1=mvb4[:, par, 0:1],
                                    scalar2=rsig4[:, par, :],
                                    op0=ALU.subtract, op1=ALU.mult)
                vinc(sv, i)
                rec[("d_evac", t_i)] = sv.n
                v.wait_ge(sv.h, sv.n)
                w = rec[("x1rd", t_i)]
                v.wait_ge(w[0].h, w[1])
                i = v.tensor_tensor(out=out_b[:, par, :], in0=ln4_b[:, par, :],
                                    in1=xs3[:, par, :], op=ALU.add)
                vinc(sv, i)
            block.vector(fvd2)
            rec[("outadd", t_i)] = sv.n
            if t_i + 2 < NT_OWN:
                l4_x1(t_i + 2)

            def fod(sync, t_i=t_i, par=par):
                sem = do_[t_i % 2]
                sync.wait_ge(sv.h, rec[("outadd", t_i)])
                sync.dma_start(out=out_d.rearrange("(q p) d -> q p d", p=P)[t_i],
                               in_=out_b[:, par, :]).then_inc(sem.h, 16)
            block.sync(fod)
            do_[t_i % 2].n += 16
            rec[("outdma", t_i)] = (do_[t_i % 2], do_[t_i % 2].n)

        s_flush()
        ctx4.close()
        ctx34.close()

        all_dma = [da[0], da[1], db, do_[0], do_[1], dsm[0], dsm[1],
                   dtm[0], dtm[1], dwA, dwB, dwC, dwP[0], dwP[1], dwF,
                   dtp[0], dtp[1], dts[0], dts[1], deG[0], deG[1],
                   deL[0], deL[1]]

        def fin(sync):
            for s_ in all_dma:
                if s_.n:
                    sync.wait_ge(s_.h, s_.n)
        block.sync(fin)

    return nc


_NC_CACHE = None


def _get_nc():
    global _NC_CACHE
    if _NC_CACHE is None:
        _NC_CACHE = build_program()
    return _NC_CACHE


def kernel(L0_attn_x, L0_attn_W_spin, L0_attn_b_spin, L0_attn_W_global,
           L0_attn_W_J, L0_attn_beta, L0_ffn_w_gate, L0_ffn_w_up,
           L0_ffn_w_down):
    x = np.asarray(L0_attn_x, np.float32)
    wsp = np.asarray(L0_attn_W_spin, np.float32)
    bsp = np.asarray(L0_attn_b_spin, np.float32)
    wgl = np.asarray(L0_attn_W_global, np.float32)
    wj = np.asarray(L0_attn_W_J, np.float32)
    beta = np.asarray(L0_attn_beta, np.float32).reshape(-1)[0]
    wg = np.asarray(L0_ffn_w_gate, np.int32)
    wu = np.asarray(L0_ffn_w_up, np.int32)
    wd = np.asarray(L0_ffn_w_down, np.int32)

    wspT = np.ascontiguousarray(wsp.T)
    wjT = np.ascontiguousarray(wj.T)
    wglT = np.ascontiguousarray(wgl.T)
    wgT = np.ascontiguousarray(wg.T)
    wuT = np.ascontiguousarray(wu.T)
    wdT = np.ascontiguousarray(wd.T)
    b_bc = np.ascontiguousarray(np.broadcast_to(bsp[None, :], (P, D)))
    beta_col = np.full((P, 1), beta, np.float32)
    ident = np.eye(P, dtype=ml_dtypes.bfloat16)

    in_maps = []
    for c in range(8):
        b, h = c // 2, c % 2
        xp = np.ascontiguousarray(
            np.concatenate([x[b, h * T:(h + 1) * T],
                            x[b, (1 - h) * T:(2 - h) * T]], 0))
        in_maps.append({
            "x": xp, "wspT": wspT, "wjT": wjT, "wglT": wglT,
            "b_bc": b_bc, "beta_col": beta_col, "ident": ident,
            "wgT": wgT, "wuT": wuT, "wdT": wdT,
        })

    nc = _get_nc()
    res = run_bass_kernel_spmd(nc, in_maps, core_ids=list(range(8)))
    out = np.empty((4, N, D), np.float32)
    for c in range(8):
        b, h = c // 2, c % 2
        out[b, h * T:(h + 1) * T] = res.results[c]["out"]
    return out
